# version 12
# baseline (speedup 1.0000x reference)
import numpy as np
import ml_dtypes
from contextlib import ExitStack

import concourse.bass as bass
import concourse.mybir as mybir
from concourse.alu_op_type import AluOpType
from concourse.bass_utils import run_bass_kernel_spmd

F32 = mybir.dt.float32
BF16 = mybir.dt.bfloat16
U16 = mybir.dt.uint16
BF16_CONSTS = {"ident_bf", "tri", "parity_mask", "ones_col_bf"}
AF = mybir.ActivationFunctionType
AX = mybir.AxisListType
OP = AluOpType

B, T, D = 4, 2048, 2048
G, PHD, DH = 4, 4, 128
KV = G * DH
NC = 8
TQ = 1024
NT, DS, GRID = 64, 32, 16
LNEPS = 1e-5
ISQD = float(1.0 / np.float32(np.sqrt(np.float32(DH))))
NEG = -1e30
CLIPHI = float(np.float32(1.0) - np.float32(1e-6))

bfdt = ml_dtypes.bfloat16
ENGS = ("sync", "gpsimd", "tensor", "vector", "scalar")

CONSTS1 = [
    ("ident_bf", [128, 128], U16),
    ("tri", [128, 128], U16), ("parity_mask", [128, 128], U16),
    ("ones_col_bf", [128, 1], U16), ("ones_row_f32", [1, 128], F32),
    ("one11", [1, 1], F32), ("ident_f32", [128, 128], F32),
    ("iota64", [128, 64], F32), ("iota8", [128, 8], F32),
    ("slopes", [NT, DS * GRID], F32),
]
IN1_SPECS = [
    ("xT", [D, T], F32), ("xq", [TQ, D], F32),
    ("Wq_t", [16, 128, 2048], F32), ("Wk_t", [4, 128, 2048], F32),
    ("Wv", [D, KV], F32), ("Wo", [D, D], F32),
    ("gam", [1, D], F32), ("bet", [1, D], F32), ("bo", [1, D], F32),
    ("bmask", [128, 8 * 512], U16),
] + CONSTS1

CONSTS2 = [
    ("ident_f32", [128, 128], F32),
    ("grid_cmp", [128, DS * GRID], F32),
    ("bases", [NT, DS * GRID], F32), ("slopes", [NT, DS * GRID], F32),
    ("oscale", [NT, 1], F32),
    ("gam_rs", [NT, DS], F32), ("bet_rs", [NT, DS], F32),
]
IN2_SPECS = [
    ("x2", [TQ, D], F32), ("partials", [128, NC], F32),
    ("xsl", [128, 8 * DS], F32), ("to_all", [128, 8 * 64], F32),
    ("onehotT", [64, 8 * 128], F32), ("stats", [128, 24], F32),
] + CONSTS2


class Prog:

    def __init__(self, nc):
        self.nc = nc
        self.es = ExitStack()
        self.sem = {}
        self.cnt = {}
        for nm in ("ld", "lx", "ls", "gx", "gk", "q0", "q1", "wo", "out",
                   "o0", "o1", "pe", "act", "dve", "gp"):
            self.sem[nm] = self.es.enter_context(nc.semaphore(nm))
            self.cnt[nm] = 0
        self.ops = {e: [] for e in ENGS}
        self.sb = {}

    def op(self, eng, fn, waits=(), inc=None, inck=1, kind="compute", drain=True):
        w = [(nm, v) for nm, v in waits if v and v > 0]
        tag = None
        if inc is not None:
            self.cnt[inc] += inck
            tag = (inc, inck)
        self.ops[eng].append((fn, w, tag, kind if drain else "nodrain"))
        return self.cnt[inc] if inc else None

    def dma(self, eng, out, in_, sem, waits=()):
        return self.op(eng, lambda q: q.dma_start(out=out, in_=in_),
                       waits=waits, inc=sem, inck=16, kind="dma")

    def alloc(self, nm, shape, dt, stack=None, side=None):
        kw = {"side": side} if side else {}
        t = (stack or self.es).enter_context(self.nc.sbuf_tensor(nm, shape, dt, **kw))
        self.sb[nm] = t
        return t

    def psum(self, nm, shape, stack=None, dt=F32):
        return (stack or self.es).enter_context(self.nc.psum_tensor(nm, shape, dt))

    def wall(self, names=("ld", "lx", "ls", "gx", "gk", "q0", "q1", "wo",
                          "pe", "act", "dve", "gp")):
        return [(nm, self.cnt[nm]) for nm in names]

    def ln_stats(self, src_tile, scr, out_mu, out_r, out_var, waits=()):
        st24, s8 = scr["st24"], scr["s8"]
        sA, sB = scr["sA"], scr["sB"]
        for cb in range(4):
            self.op("vector", lambda q, cb=cb: q.bn_stats(
                st24[:, cb * 6:(cb + 1) * 6], src_tile[:, cb * 512:(cb + 1) * 512]),
                waits=waits if cb == 0 else ())
        tri = st24[:, :].rearrange("p (a b) -> p a b", a=8)
        mus = tri[:, :, 1:2].squeeze()
        m2s = tri[:, :, 2:3].squeeze()
        self.op("vector", lambda q: q.tensor_tensor(s8[...], mus, mus, OP.mult))
        self.op("vector", lambda q: q.tensor_reduce(sA[...], s8[...], AX.X, OP.add))
        self.op("vector", lambda q: q.tensor_reduce(sB[...], m2s, AX.X, OP.add))
        self.op("vector", lambda q: q.scalar_tensor_tensor(
            sB[...], sA[...], 256.0, sB[...], OP.mult, OP.add))
        self.op("vector", lambda q: q.tensor_reduce(out_mu, mus, AX.X, OP.add))
        self.op("vector", lambda q: q.tensor_single_scalar(
            out_mu, out_mu, 1.0 / 8.0, OP.mult))
        self.op("vector", lambda q: q.tensor_tensor(sA[...], out_mu, out_mu, OP.mult))
        self.op("vector", lambda q: q.tensor_single_scalar(
            out_var, sB[...], 1.0 / 2048.0, OP.mult))
        v = self.op("vector", lambda q: q.tensor_tensor(
            out_var, out_var, sA[...], OP.subtract), inc="dve")
        a = self.op("scalar", lambda q: q.activation(
            out=out_r, in_=out_var, func=AF.Sqrt, bias=scr["eps"][:, :]),
            waits=[("dve", v)], inc="act")
        r = self.op("vector", lambda q: q.reciprocal(out_r, out_r),
                    waits=[("act", a)], inc="dve")
        return r

    def ln_stats8(self, x2t, ST, S64, MU8, R8, V8, SA8, SB8, eps, waits=()):
        first = True
        for tb in range(8):
            for cb in range(4):
                self.op("vector", lambda q, tb=tb, cb=cb: q.bn_stats(
                    ST[:, tb * 24 + cb * 6: tb * 24 + (cb + 1) * 6],
                    x2t(tb)[:, cb * 512:(cb + 1) * 512]),
                    waits=waits if first else (), drain=False)
                first = False
        self.op("vector", lambda q: q.nop())
        v4 = ST[:, :].rearrange("p (t a b) -> p t a b", t=8, a=8)
        mus = v4[:, :, :, 1:2].squeeze()
        m2s = v4[:, :, :, 2:3].squeeze()
        self.op("vector", lambda q: q.tensor_tensor(S64[...], mus, mus, OP.mult))
        self.op("vector", lambda q: q.tensor_reduce(
            SA8[...], S64[...].rearrange("p (t a) -> p t a", t=8), AX.X, OP.add))
        self.op("vector", lambda q: q.tensor_reduce(SB8[...], m2s, AX.X, OP.add))
        self.op("vector", lambda q: q.scalar_tensor_tensor(
            SB8[...], SA8[...], 256.0, SB8[...], OP.mult, OP.add))
        self.op("vector", lambda q: q.tensor_reduce(MU8[...], mus, AX.X, OP.add))
        self.op("vector", lambda q: q.tensor_single_scalar(
            MU8[...], MU8[...], 1.0 / 8.0, OP.mult))
        self.op("vector", lambda q: q.tensor_tensor(SA8[...], MU8[...], MU8[...],
                                                    OP.mult))
        self.op("vector", lambda q: q.tensor_single_scalar(
            V8[...], SB8[...], 1.0 / 2048.0, OP.mult))
        v = self.op("vector", lambda q: q.tensor_tensor(
            V8[...], V8[...], SA8[...], OP.subtract), inc="dve")
        a = self.op("scalar", lambda q: q.activation(
            out=R8[...], in_=V8[...], func=AF.Sqrt, bias=eps[:, :]),
            waits=[("dve", v)], inc="act")
        r = self.op("vector", lambda q: q.reciprocal(R8[...], R8[...]),
                    waits=[("act", a)], inc="dve")
        return r

    def emit(self):
        with self.nc.Block() as blk:
            for eng in ENGS:
                seq = self.ops[eng]
                if not seq:
                    continue

                def section(q, seq=seq, eng=eng):
                    autodrain = eng in ("vector", "scalar")
                    for fn, waits, tag, kind in seq:
                        for nm, v in waits:
                            q.wait_ge(self.sem[nm], v)
                        instr = fn(q)
                        if autodrain and kind == "compute":
                            instr = q.drain()
                        if tag is not None:
                            instr.then_inc(self.sem[tag[0]], tag[1])

                getattr(blk, eng)(section)


def build_launch1(debug=(), upto=9):
    nc = bass.Bass("TRN2", target_bir_lowering=False, detect_race_conditions=False)
    din = {nm: nc.dram_tensor(nm, shape, dt, kind="ExternalInput").ap()
           for nm, shape, dt in IN1_SPECS}
    dout = {nm: nc.dram_tensor(nm, shape, F32, kind="ExternalOutput").ap()
            for nm, shape in [
                ("o_partials", [128, 1]), ("o_x2", [TQ, D]),
                ("o_xsl", [128, 8 * DS]), ("o_to", [128, 8 * 64]),
                ("o_onehotT", [64, 8 * 128]), ("o_stats", [128, 24]),
            ]}
    dbg = {}
    ssrow_dram = nc.dram_tensor("ssrow_dram", [NT, DS], F32, kind="Internal")

    P = Prog(nc)
    sb = P.sb
    epsc = P.alloc("epsc", [128, 1], F32)
    P.op("vector", lambda q: q.memset(epsc[:, :], LNEPS), inc="dve")

    def tap(nm, src_ap, shape, waits):
        if nm not in debug:
            return
        dbg[nm] = nc.dram_tensor("dbg_" + nm, shape, F32, kind="ExternalOutput").ap()
        P.dma("gpsimd", dbg[nm], src_ap, "out", waits=waits)

    for nm, shape, dt in CONSTS1:
        P.alloc("c_" + nm, shape, dt)
    C = lambda nm: (sb["c_" + nm][...].bitcast(BF16) if nm in BF16_CONSTS
                    else sb["c_" + nm][...])
    gB = P.alloc("gB", [128, D], F32)
    bB = P.alloc("bB", [128, D], F32)
    ssB = P.alloc("ssB", [128, D], F32)
    attn_es = ExitStack()
    KBF = P.alloc("KBF", [128, 4 * 2048], BF16, stack=attn_es)
    VBF = P.alloc("VBF", [128, 16 * 512], BF16, stack=attn_es)
    QBF = P.alloc("QBF", [128, 16 * 1024], BF16, stack=attn_es)
    rows_es = ExitStack()
    for nm, shape, dt in [
            ("negmu_bf", [1, D], BF16), ("rB", [128, D], F32),
            ("rcol", [128, 16], F32), ("wqsum", [1, 2048], BF16),
            ("wksum", [1, 512], BF16), ("wvsum", [1, 512], BF16)]:
        P.alloc(nm, shape, dt, stack=rows_es)

    for nm, shape, dt in CONSTS1:
        P.dma("sync", sb["c_" + nm][...], din[nm], "ld")
    ld_consts = P.cnt["ld"]

    xfm_es = ExitStack()
    XFM = P.alloc("XFM", [128, 16 * 2048], BF16, stack=xfm_es, side="right")
    xfm = lambda dc: XFM[:, dc * 2048:(dc + 1) * 2048]
    xld = []
    for dc in range(16):
        sem = "gx" if dc % 2 == 0 else "gk"
        w = [(sem, P.cnt[sem])] if dc >= 2 else []
        c = P.dma("gpsimd", xfm(dc), din["xT"][dc * 128:(dc + 1) * 128, :],
                  sem, waits=w)
        xld.append((sem, c))

    ph1 = ExitStack()
    for nm in ("murow", "rrow", "row2"):
        P.alloc(nm, [1, D], F32, stack=ph1, side="right")
    xsq = [P.alloc("XSQ0", [128, 2048], BF16, stack=ph1, side="right")] * 2
    ph1a = ExitStack()
    ps_mu = [P.psum(f"ps_mu{i}", [1, 512], stack=ph1a) for i in range(4)]
    ps_sq = [P.psum(f"ps_sq{i}", [1, 512], stack=ph1a) for i in range(4)]

    sq_done, sq_pe = [], []
    for dc in range(16):
        w = [xld[dc]]
        if dc >= 1:
            w.append(("pe", sq_pe[dc - 1]))
        sq_done.append(P.op("scalar", lambda q, dc=dc: q.activation(
            out=xsq[dc % 2][...], in_=xfm(dc), func=AF.Square), waits=w, inc="act",
            drain=False))
        for cb in range(4):
            P.op("tensor", lambda q, dc=dc, cb=cb: q.matmul(
                ps_mu[cb][...], C("ones_col_bf"),
                xfm(dc)[:, cb * 512:(cb + 1) * 512],
                start=(dc == 0), stop=(dc == 15)),
                waits=[xld[dc], ("ld", ld_consts)] if cb == 0 else ())
        for cb in range(4):
            v = P.op("tensor", lambda q, dc=dc, cb=cb: q.matmul(
                ps_sq[cb][...], C("ones_col_bf"),
                xsq[dc % 2][:, cb * 512:(cb + 1) * 512],
                start=(dc == 0), stop=(dc == 15)),
                waits=[("act", sq_done[dc])] if cb == 0 else (),
                inc="pe" if cb == 3 else None)
        sq_pe.append(v)
    pe_stats = P.cnt["pe"]

    for cb in range(4):
        P.op("scalar", lambda q, cb=cb: q.activation(
            out=sb["murow"][0:1, cb * 512:(cb + 1) * 512], in_=ps_mu[cb][...],
            func=AF.Identity, scale=1.0 / D),
            waits=[("pe", pe_stats)] if cb == 0 else (), drain=False)
    for cb in range(4):
        P.op("scalar", lambda q, cb=cb: q.activation(
            out=sb["rrow"][0:1, cb * 512:(cb + 1) * 512], in_=ps_sq[cb][...],
            func=AF.Identity, scale=1.0 / D),
            inc="act" if cb == 3 else None)
    act_ex2 = P.cnt["act"]

    P.op("vector", lambda q: q.tensor_tensor(
        sb["row2"][...], sb["murow"][...], sb["murow"][...], OP.mult),
        waits=[("act", act_ex2)])
    P.op("vector", lambda q: q.tensor_tensor(
        sb["row2"][...], sb["rrow"][...], sb["row2"][...], OP.subtract), inc="dve")
    dve_var = P.cnt["dve"]
    P.op("scalar", lambda q: q.activation(out=sb["row2"][...], in_=sb["row2"][...],
                                          func=AF.Sqrt, bias=epsc[0:1, 0:1]),
         waits=[("dve", dve_var)], inc="act")
    act_sqrt = P.cnt["act"]
    P.op("vector", lambda q: q.reciprocal(sb["rrow"][...], sb["row2"][...]),
         waits=[("act", act_sqrt)])
    P.op("vector", lambda q: q.tensor_single_scalar(
        sb["row2"][...], sb["murow"][...], -1.0, OP.mult))
    P.op("vector", lambda q: q.tensor_copy(sb["negmu_bf"][...], sb["row2"][...]),
         inc="dve")
    dve_rows = P.cnt["dve"]

    ph1a.close()
    ps_bc = [P.psum(f"ps_bc{i}", [128, 512], stack=ph1) for i in range(2)]
    ps_t = P.psum("ps_t", [128, 1], stack=ph1)
    bc_hist = []

    def bcast_rows(dst, src_row, nch, extra_w=()):
        for cb in range(nch):
            bi = len(bc_hist) % 2
            w = list(extra_w) if cb == 0 else []
            if len(bc_hist) >= 2:
                w.append(("dve", bc_hist[-2]))
            pe_v = P.op("tensor", lambda q, cb=cb, src_row=src_row, bi=bi: q.matmul(
                ps_bc[bi][...], C("ones_row_f32"),
                src_row[0:1, cb * 512:(cb + 1) * 512], start=True, stop=True),
                waits=w, inc="pe")
            bc_hist.append(P.op(
                "vector", lambda q, cb=cb, dst=dst, bi=bi:
                q.tensor_copy(dst[:, cb * 512:(cb + 1) * 512], ps_bc[bi][...]),
                waits=[("pe", pe_v)], inc="dve", drain=False))
        return bc_hist[-1]

    bcast_rows(sb["rB"][...], sb["rrow"], 4, extra_w=[("dve", dve_rows)])

    tr_prev = dve_rows
    for tb in range(16):
        pe_v = P.op("tensor", lambda q, tb=tb: q.transpose(
            ps_t[...], sb["rrow"][0:1, tb * 128:(tb + 1) * 128], C("one11")),
            waits=[("dve", tr_prev)], inc="pe")
        tr_prev = P.op("vector", lambda q, tb=tb: q.tensor_copy(
            sb["rcol"][:, tb:tb + 1], ps_t[...]), waits=[("pe", pe_v)], inc="dve")
    ph1_psum_readers = P.cnt["dve"]
    ph1.close()

    ph2 = ExitStack()
    KW = P.alloc("KW", [128, 4 * 2048], BF16, stack=ph2, side="right")
    VW = P.alloc("VW", [128, 16 * 512], BF16, stack=ph2, side="right")
    for ic in range(4):
        P.dma("gpsimd", KW[:, ic * 2048:(ic + 1) * 2048], din["Wk_t"][ic], "gk",
              waits=P.wall() if ic == 0 else ())
    for dc in range(16):
        P.dma("gpsimd", VW[:, dc * 512:(dc + 1) * 512],
              din["Wv"][dc * 128:(dc + 1) * 128, :], "gk")
    gk_all = P.cnt["gk"]

    ps_w = P.psum("ps_w", [1, 512], stack=ph2)
    ps_p = [P.psum(f"ps_p{i}", [128, 512], stack=ph2) for i in range(2)]

    gate2 = P.wall()
    for ic in range(4):
        for dc in range(16):
            w0 = []
            if dc == 0:
                w0 = (gate2 + [("gk", gk_all)]) if ic == 0 \
                    else [("act", P.cnt["act"])]
            v = P.op("tensor", lambda q, ic=ic, dc=dc: q.matmul(
                ps_w[0:1, 0:128], C("ones_col_bf"),
                KW[:, ic * 2048 + dc * 128: ic * 2048 + (dc + 1) * 128],
                start=(dc == 0), stop=(dc == 15)),
                waits=w0,
                inc="pe" if dc == 15 else None)
        P.op("scalar", lambda q, ic=ic: q.activation(
            out=sb["wksum"][0:1, ic * 128:(ic + 1) * 128], in_=ps_w[0:1, 0:128],
            func=AF.Identity), waits=[("pe", v)], inc="act", drain=False)
    for dc in range(16):
        v = P.op("tensor", lambda q, dc=dc: q.matmul(
            ps_w[...], C("ones_col_bf"), VW[:, dc * 512:(dc + 1) * 512],
            start=(dc == 0), stop=(dc == 15)),
            waits=[("gk", gk_all), ("act", P.cnt["act"])] if dc == 0 else (),
            inc="pe" if dc == 15 else None)
    P.op("scalar", lambda q: q.activation(out=sb["wvsum"][...], in_=ps_w[...],
                                          func=AF.Identity),
         waits=[("pe", v)], inc="act", drain=False)
    act_ws = P.cnt["act"]

    kv_evac = []
    pi = 0
    for ic in range(4):
        for tb4 in range(4):
            w = [("act", act_ws), ("dve", ph1_psum_readers)]
            if len(kv_evac) >= 2:
                w.append(("dve", kv_evac[-2]))
            for dc in range(16):
                P.op("tensor", lambda q, ic=ic, tb4=tb4, dc=dc, pi=pi: q.matmul(
                    ps_p[pi][...],
                    KW[:, ic * 2048 + dc * 128: ic * 2048 + (dc + 1) * 128],
                    xfm(dc)[:, tb4 * 512:(tb4 + 1) * 512],
                    start=(dc == 0), stop=False),
                    waits=w if dc == 0 else ())
                w = ()
            pe_v = P.op("tensor", lambda q, ic=ic, tb4=tb4, pi=pi: q.matmul(
                ps_p[pi][...], sb["wksum"][0:1, ic * 128:(ic + 1) * 128],
                sb["negmu_bf"][0:1, tb4 * 512:(tb4 + 1) * 512],
                start=False, stop=True), inc="pe")
            kv_evac.append(P.op("vector", lambda q, ic=ic, tb4=tb4, pi=pi:
                                q.tensor_tensor(
                                    KBF[:, ic * 2048 + tb4 * 512: ic * 2048 + (tb4 + 1) * 512],
                                    ps_p[pi][...],
                                    sb["rB"][:, tb4 * 512:(tb4 + 1) * 512], OP.mult),
                                waits=[("pe", pe_v)], inc="dve", drain=False))
            pi ^= 1

    for tb in range(16):
        w = []
        if len(kv_evac) >= 2:
            w.append(("dve", kv_evac[-2]))
        for dc in range(16):
            P.op("tensor", lambda q, tb=tb, dc=dc, pi=pi: q.matmul(
                ps_p[pi][...], xfm(dc)[:, tb * 128:(tb + 1) * 128],
                VW[:, dc * 512:(dc + 1) * 512], start=(dc == 0), stop=False),
                waits=w if dc == 0 else ())
            w = ()
        pe_v = P.op("tensor", lambda q, tb=tb, pi=pi: q.matmul(
            ps_p[pi][...], sb["negmu_bf"][0:1, tb * 128:(tb + 1) * 128],
            sb["wvsum"][0:1, :], start=False, stop=True), inc="pe")
        kv_evac.append(P.op("vector", lambda q, tb=tb, pi=pi: q.tensor_scalar(
            VBF[:, tb * 512:(tb + 1) * 512], ps_p[pi][...],
            sb["rcol"][:, tb:tb + 1], None, OP.mult, OP.bypass),
            waits=[("pe", pe_v)], inc="dve", drain=False))
        pi ^= 1
    ph2_dve = P.cnt["dve"]
    ph2.close()

    ph3 = ExitStack()
    qwb = [P.alloc(f"qwb{i}", [128, 2048], BF16, stack=ph3, side="right")
           for i in range(2)]
    ps_q = [P.psum(f"ps_q{i}", [128, 512], stack=ph3) for i in range(2)]
    ps_w3 = P.psum("ps_w3", [1, 512], stack=ph3)

    def even_view(ap2d, base):
        v = ap2d[:, base:base + 1024]
        v = v.rearrange("p (k o) -> p k o", k=4)
        return v[:, :, 0:128]

    qw_ld, q_evac, qsum_pe, q_pe_round = [], [], [], []
    gate3 = P.wall()
    for ic in range(16):
        bufi = ic % 2
        wld = [("pe", q_pe_round[ic - 2])] if ic >= 2 else (
            P.wall() if ic == 0 else [])
        qw_ld.append(P.dma("gpsimd", qwb[bufi][...], din["Wq_t"][ic],
                           "q%d" % bufi, waits=wld))
        for dc in range(16):
            w0 = []
            if dc == 0:
                w0 = [("q%d" % bufi, 16 * (ic // 2 + 1))]
                w0 += gate3 if ic == 0 else [("act", P.cnt["act"])]
            v = P.op("tensor", lambda q, bufi=bufi, dc=dc: q.matmul(
                ps_w3[0:1, 0:128], C("ones_col_bf"),
                qwb[bufi][:, dc * 128:(dc + 1) * 128],
                start=(dc == 0), stop=(dc == 15)),
                waits=w0,
                inc="pe" if dc == 15 else None)
        qsum_pe.append(v)
        P.op("scalar", lambda q, ic=ic: q.activation(
            out=sb["wqsum"][0:1, ic * 128:(ic + 1) * 128], in_=ps_w3[0:1, 0:128],
            func=AF.Identity), waits=[("pe", qsum_pe[ic])], inc="act", drain=False)
        act_qs = P.cnt["act"]
        for qh in range(2):
            pqi = qh
            w = [("act", act_qs)]
            if len(q_evac) >= 2:
                w.append(("dve", q_evac[-2]))
            for dc in range(16):
                P.op("tensor", lambda q, bufi=bufi, dc=dc, qh=qh, pqi=pqi: q.matmul(
                    ps_q[pqi][...], qwb[bufi][:, dc * 128:(dc + 1) * 128],
                    even_view(xfm(dc), qh * 1024),
                    start=(dc == 0), stop=False),
                    waits=w if dc == 0 else ())
                w = ()
            pe_v = P.op("tensor", lambda q, ic=ic, qh=qh, pqi=pqi: q.matmul(
                ps_q[pqi][...], sb["wqsum"][0:1, ic * 128:(ic + 1) * 128],
                even_view(sb["negmu_bf"][...], qh * 1024),
                start=False, stop=True), inc="pe")
            q_evac.append(P.op("vector", lambda q, ic=ic, qh=qh, pqi=pqi:
                               q.tensor_tensor(
                                   QBF[:, ic * 1024 + qh * 512: ic * 1024 + (qh + 1) * 512],
                                   ps_q[pqi][...],
                                   even_view(sb["rB"][...], qh * 1024), OP.mult),
                               waits=[("pe", pe_v)], inc="dve", drain=False))
        q_pe_round.append(pe_v)
    ph3.close()

    tap("QBF", QBF[...], [128, 16 * 1024], P.wall())
    tap("KBF", KBF[...], [128, 4 * 2048], P.wall())
    tap("VBF", VBF[...], [128, 16 * 512], P.wall())

    xfm_es.close()
    rows_es.close()

    gb_es = ExitStack()
    ssmat = P.alloc("ssmat", [64, DS], F32, stack=gb_es, side="right")
    gate_f = P.wall()
    P.dma("sync", gB[...],
          din["gam"].rearrange("o d -> (o d)").partition_broadcast(128), "ls")
    P.dma("sync", bB[...],
          din["bet"].rearrange("o d -> (o d)").partition_broadcast(128), "ls")
    P.op("vector", lambda q: q.tensor_reduce(
        ssmat[...], C("slopes").rearrange("p (j g) -> p j g", j=DS), AX.X, OP.add),
        waits=gate_f)
    sv = P.op("vector", lambda q: q.nop(), inc="dve")
    sa = P.op("scalar", lambda q: q.activation(out=ssmat[...], in_=ssmat[...],
                                               func=AF.Sign),
              waits=[("dve", sv)], inc="act")
    d1 = P.dma("sync", ssrow_dram.ap(), ssmat[...], "ls", waits=[("act", sa)])
    bc_last_ls = P.dma(
        "sync", ssB[...],
        ssrow_dram.ap().rearrange("a b -> (a b)").partition_broadcast(128),
        "ls", waits=[("ls", d1)])
    gb_es.close()

    if upto <= 3:
        P.op("sync", lambda q: q.nop(), waits=[("out", P.cnt["out"])] + P.wall())
        P.emit()
        return nc, din, dout, dbg

    ph45 = ExitStack()
    ST8 = P.alloc("ST8", [128, 8 * 24], F32, stack=ph45, side="right")
    S64 = P.alloc("S64", [128, 64], F32, stack=ph45, side="right")
    M8 = {nm: P.alloc(nm, [128, 8], F32, stack=ph45, side="right")
          for nm in ("MU8", "R8", "V8", "SA8", "SB8", "BA8", "RHO8",
                     "TKN8", "TKX8")}
    sc1_es = ExitStack()
    SC1x = [P.alloc(f"SC1x{i}", [128, 2048], F32, stack=sc1_es, side="right")
            for i in range(2)]
    ctx_es = ExitStack()
    CTX = P.alloc("CTX", [128, 16 * 1024], BF16, stack=ctx_es, side="right")
    boB = P.alloc("boB", [128, D], F32, stack=ctx_es, side="right")
    WOB = P.alloc("WOB", [128, 16 * 1024], BF16, stack=ctx_es, side="right")
    ph4 = ExitStack()
    A3 = [P.alloc(f"A3{i}", [128, 512], BF16, stack=ph4, side="right")
          for i in range(3)]
    rs_row = [P.alloc(f"rs_row{i}", [1, 512], F32, stack=ph4, side="right")
              for i in range(2)]
    rb_ctx = [P.alloc(f"rb_ctx{i}", [128, 512], F32, stack=ph4, side="right")
              for i in range(2)]
    BM = P.alloc("BM", [128, 8 * 512], U16, stack=ph4, side="right")
    ps_s = [P.psum(f"ps_s{i}", [128, 512], stack=ph4) for i in range(3)]
    ps_ctx = [P.psum(f"ps_ctx{i}", [128, 512], stack=ph4) for i in range(2)]
    ps_sum = [P.psum(f"ps_sum{i}", [1, 512], stack=ph4) for i in range(2)]
    ps_rb = P.psum("ps_rb", [128, 512], stack=ph4)

    gate4 = P.wall()
    gate3e = [("pe", P.cnt["pe"]), ("act", P.cnt["act"]),
              ("dve", P.cnt["dve"]), ("gp", P.cnt["gp"])]
    bm_ld1 = P.dma("sync", BM[:, 0:1024], din["bmask"][:, 0:1024], "ld",
                   waits=list(gate3e))
    bm_ld2 = P.dma("sync", BM[:, 1024:4096], din["bmask"][:, 1024:4096], "ld")
    bo_ls = P.dma("sync", boB[...],
                  din["bo"].rearrange("o d -> (o d)").partition_broadcast(128),
                  "ls", waits=list(gate3e))
    for hc in range(16):
        P.dma("gpsimd", WOB[:, hc * 1024:(hc + 1) * 1024],
              din["Wo"][hc * 128:(hc + 1) * 128, 0:1024],
              "wo", waits=list(gate3e) if hc == 0 else [])

    exp_done = {}
    mask_gp = {}
    ctx_pe = {}
    recip_d = {-2: 0, -1: 0}
    rbcopy_a = {-1: 0}
    evac_d = {-2: 0, -1: 0}
    groups = [(4 * g + p, qg)
              for g in range(G) for p in range(PHD) for qg in range(2)]
    tiles = []
    for n, (h, qg) in enumerate(groups):
        n_nb = 8 * qg
        tau = n_nb + 8
        for t in range(tau):
            tiles.append((n, h, qg, t, tau, t if t < n_nb else None,
                          None if t < n_nb else t - n_nb))
    NTI = len(tiles)

    def plan_group_tail(n):
        rb_v = P.op("tensor", lambda q, n=n: q.matmul(
            ps_rb[...], C("ones_row_f32"), rs_row[n % 2][...],
            start=True, stop=True),
            waits=[("dve", recip_d[n]), ("act", rbcopy_a[n - 1])], inc="pe")
        rbcopy_a[n] = P.op("scalar", lambda q, n=n: q.activation(
            out=rb_ctx[n % 2][...], in_=ps_rb[...], func=AF.Identity),
            waits=[("pe", rb_v), ("dve", evac_d[n - 2])], inc="act", drain=False)
        h, qg = groups[n]
        evac_d[n] = P.op("vector", lambda q, h=h, qg=qg, n=n: q.tensor_tensor(
            CTX[:, h * 1024 + qg * 512: h * 1024 + (qg + 1) * 512],
            ps_ctx[n % 2][...], rb_ctx[n % 2][...], OP.mult),
            waits=[("act", rbcopy_a[n])], inc="dve", drain=False)

    def plan_score_exp(i):
        n, h, qg, t, tau, kb_nb, kbp = tiles[i]
        g = h // 4
        qbase = h * 1024 + qg * 512
        kb = kb_nb if kbp is None else 8 * qg + kbp
        si = i % 3
        w = list(gate4) if i == 0 else []
        if i >= 3:
            w.append(("act", exp_done[i - 3]))
        pe_v = P.op("tensor", lambda q, g=g, kb=kb, qbase=qbase, si=si:
                    q.matmul(ps_s[si][...],
                             KBF[:, g * 2048 + kb * 128: g * 2048 + (kb + 1) * 128],
                             QBF[:, qbase:qbase + 512],
                             start=True, stop=True),
                    waits=w, inc="pe")
        exp_done[i] = P.op("scalar", lambda q, si=si, ab=si:
                           q.activation(out=A3[ab][...], in_=ps_s[si][...],
                                        func=AF.Exp, scale=ISQD),
                           waits=[("pe", pe_v)], inc="act", drain=False)
        if kbp is not None:
            mask_gp[i] = P.op("gpsimd", lambda q, ab=si, kbp=kbp: q.tensor_mul(
                A3[ab][...], A3[ab][...],
                BM[:, kbp * 512:(kbp + 1) * 512].bitcast(BF16)),
                waits=[("act", exp_done[i]),
                       ("ld", bm_ld1 if kbp < 2 else bm_ld2)], inc="gp")

    def plan_sum_ctx(i):
        n, h, qg, t, tau, kb_nb, kbp = tiles[i]
        g = h // 4
        kb = kb_nb if kbp is None else 8 * qg + kbp
        ab = i % 3
        ws = [("gp", mask_gp[i])] if kbp is not None else [("act", exp_done[i])]
        if t == 0:
            ws.append(("dve", recip_d[n - 2]))
        P.op("tensor", lambda q, ab=ab, t=t, tau=tau, n=n: q.matmul(
            ps_sum[n % 2][...], C("ones_col_bf"), A3[ab][...],
            start=(t == 0), stop=(t == tau - 1)), waits=ws)
        wc = [("dve", evac_d[n - 2])] if t == 0 else ()
        ctx_pe[i] = P.op("tensor", lambda q, g=g, kb=kb, ab=ab, t=t,
                         tau=tau, n=n: q.matmul(
            ps_ctx[n % 2][...],
            VBF[:, kb * 512 + g * 128: kb * 512 + (g + 1) * 128],
            A3[ab][...], start=(t == 0), stop=(t == tau - 1)),
            waits=wc, inc="pe")
        if t == tau - 1:
            recip_d[n] = P.op("vector", lambda q, n=n: q.reciprocal(
                rs_row[n % 2][...], ps_sum[n % 2][...]),
                waits=[("pe", P.cnt["pe"])], inc="dve")
            if n >= 1:
                plan_group_tail(n - 1)

    for i in range(NTI):
        plan_score_exp(i)
        if i >= 2:
            plan_sum_ctx(i - 2)
    plan_sum_ctx(NTI - 2)
    plan_sum_ctx(NTI - 1)
    plan_group_tail(len(groups) - 1)
    ph4.close()
    attn_es.close()
    tap("CTX", CTX[...], [128, 16 * 1024], P.wall())
    if upto <= 4:
        P.op("sync", lambda q: q.nop(), waits=[("out", P.cnt["out"])] + P.wall())
        P.emit()
        return nc, din, dout, dbg

    X2 = P.alloc("X2", [128, 8 * 2048], F32)
    x2t = lambda tb: X2[:, tb * 2048:(tb + 1) * 2048]
    XSL = P.alloc("XSL", [128, 8 * DS], F32)
    TOA = P.alloc("TOA", [128, 8 * 64], F32)
    OHT = P.alloc("OHT", [64, 8 * 128], F32)
    STK = P.alloc("STK", [128, 24], F32)
    TMIN = P.alloc("TMIN", [128, 64], F32)
    TMAX = P.alloc("TMAX", [128, 64], F32)
    SEND = P.alloc("SEND", [128, 1], F32)

    ph5 = ExitStack()
    XQ = [P.alloc(f"XQ{i}", [128, 2048], F32, stack=ph5, side="right")
          for i in range(2)]
    ps_o = [P.psum(f"ps_o{i}", [128, 512], stack=ph5) for i in range(2)]
    gate5 = [("pe", P.cnt["pe"]), ("act", P.cnt["act"]),
             ("dve", P.cnt["dve"]), ("gp", P.cnt["gp"])]

    MU8, R8, V8 = M8["MU8"][...], M8["R8"][...], M8["V8"][...]
    SA8, SB8, BA8 = M8["SA8"][...], M8["SB8"][...], M8["BA8"][...]

    def bn1_tb(x2t_, tb, waits):
        for cb in range(4):
            P.op("vector", lambda q, tb=tb, cb=cb: q.bn_stats(
                ST8[:, tb * 24 + cb * 6: tb * 24 + (cb + 1) * 6],
                x2t_(tb)[:, cb * 512:(cb + 1) * 512]),
                waits=waits if cb == 0 else (), drain=False)

    def agg_half(h, waits=()):
        sl = slice(h * 4, h * 4 + 4)
        v4 = ST8[:, h * 96:(h + 1) * 96].rearrange(
            "p (t a b) -> p t a b", t=4, a=8)
        mus = v4[:, :, :, 1:2].squeeze()
        m2s = v4[:, :, :, 2:3].squeeze()
        S64h = S64[:, h * 32:(h + 1) * 32]
        P.op("vector", lambda q: q.nop(), waits=waits)
        P.op("vector", lambda q: q.tensor_tensor(S64h, mus, mus, OP.mult))
        P.op("vector", lambda q: q.tensor_reduce(
            SA8[:, sl], S64h.rearrange("p (t a) -> p t a", t=4), AX.X, OP.add))
        P.op("vector", lambda q: q.tensor_reduce(SB8[:, sl], m2s, AX.X, OP.add))
        P.op("vector", lambda q: q.scalar_tensor_tensor(
            SB8[:, sl], SA8[:, sl], 256.0, SB8[:, sl], OP.mult, OP.add))
        P.op("vector", lambda q: q.tensor_reduce(MU8[:, sl], mus, AX.X, OP.add))
        P.op("vector", lambda q: q.tensor_single_scalar(
            MU8[:, sl], MU8[:, sl], 1.0 / 8.0, OP.mult))
        P.op("vector", lambda q: q.tensor_tensor(SA8[:, sl], MU8[:, sl],
                                                 MU8[:, sl], OP.mult))
        P.op("vector", lambda q: q.tensor_single_scalar(
            V8[:, sl], SB8[:, sl], 1.0 / 2048.0, OP.mult))
        v = P.op("vector", lambda q: q.tensor_tensor(
            V8[:, sl], V8[:, sl], SA8[:, sl], OP.subtract), inc="dve")
        a = P.op("scalar", lambda q: q.activation(
            out=R8[:, sl], in_=V8[:, sl], func=AF.Sqrt, bias=sb["epsc"][:, :]),
            waits=[("dve", v)], inc="act")
        return P.op("vector", lambda q: q.reciprocal(R8[:, sl], R8[:, sl]),
                    waits=[("act", a)], inc="dve")

    xq_ld = []
    res_gp = []

    def plan_res(tb, bvh):
        wl = [("gp", res_gp[tb - 2])] if tb >= 2 else gate5
        xq_ld.append(P.dma("sync", XQ[tb % 2][...],
                           din["xq"][tb * 128:(tb + 1) * 128, :], "lx", waits=wl))
        wsc = [("dve", bvh)]
        if tb >= 2:
            wsc.append(("gp", res_gp[tb - 2]))
        sc1_act = P.op("scalar", lambda q, tb=tb: q.activation(
            out=SC1x[tb % 2][...], in_=x2t(tb), func=AF.Identity,
            scale=M8["R8"][:, tb:tb + 1], bias=M8["BA8"][:, tb:tb + 1]),
            waits=wsc, inc="act", drain=False)
        res_gp.append(P.op("gpsimd", lambda q, tb=tb: q.tensor_add(
            x2t(tb), SC1x[tb % 2][...], XQ[tb % 2][...]),
            waits=[("act", sc1_act), ("lx", xq_ld[tb])], inc="gp"))
        P.dma("sync", dout["o_x2"].rearrange("(a p) d -> p a d", p=128)
              [:, tb:tb + 1, :],
              x2t(tb).rearrange("p (o d) -> p o d", o=1),
              "out", waits=[("gp", res_gp[tb])])

    def stats1_half(h):
        rv = agg_half(h)
        P.op("vector", lambda q, h=h: q.tensor_tensor(
            BA8[:, h * 4:(h + 1) * 4], MU8[:, h * 4:(h + 1) * 4],
            R8[:, h * 4:(h + 1) * 4], OP.mult))
        bvh = P.op("vector", lambda q, h=h: q.tensor_single_scalar(
            BA8[:, h * 4:(h + 1) * 4], BA8[:, h * 4:(h + 1) * 4], -1.0,
            OP.mult), inc="dve")
        for tb in range(h * 4, h * 4 + 4):
            plan_res(tb, bvh)

    o_evac = {}
    oi = 0
    qsems = ["wo", "q0", "q1", "gx"]
    wq_cnt = {}
    for jh in range(2):
        if jh == 1:
            for hc in range(16):
                wld = []
                if hc == 0:
                    wld.append(("pe", P.cnt["pe"]))
                sem = qsems[hc // 4]
                P.dma("gpsimd", WOB[:, hc * 1024:(hc + 1) * 1024],
                      din["Wo"][hc * 128:(hc + 1) * 128,
                                jh * 1024:(jh + 1) * 1024],
                      sem, waits=wld)
                wq_cnt[hc // 4] = P.cnt[sem]
        for tb in range(8):
            for jb in range(2):
                si = oi % 2
                w = [("wo", 256)] if jh == 0 else []
                if oi >= 2:
                    w.append(("dve", o_evac[oi - 2]))
                for hc in range(16):
                    if jh == 1 and (tb, jb) == (0, 0) and hc % 4 == 0:
                        w.append((qsems[hc // 4], wq_cnt[hc // 4]))
                    v = P.op("tensor", lambda q, hc=hc, tb=tb, jb=jb, si=si: q.matmul(
                        ps_o[si][...],
                        CTX[:, hc * 1024 + tb * 128: hc * 1024 + (tb + 1) * 128],
                        WOB[:, hc * 1024 + jb * 512: hc * 1024 + (jb + 1) * 512],
                        start=(hc == 0), stop=(hc == 15)),
                        waits=w if (hc == 0 or w) else (),
                        inc="pe" if hc == 15 else None)
                    w = []
                jcol = jh * 1024 + jb * 512
                o_evac[oi] = P.op("vector", lambda q, tb=tb, jcol=jcol, si=si:
                                  q.tensor_tensor(
                                      x2t(tb)[:, jcol:jcol + 512], ps_o[si][...],
                                      boB[:, jcol:jcol + 512], OP.add),
                                  waits=[("pe", v)] +
                                  ([("ls", bo_ls)] if oi == 0 else []),
                                  inc="dve", drain=False)
                oi += 1
            if jh == 1:
                bn1_tb(x2t, tb, [("dve", o_evac[oi - 1])])
                if tb == 3:
                    stats1_half(0)
    stats1_half(1)
    o_all = P.cnt["dve"]
    if upto <= 4.7:
        tap("X2", X2[...], [128, 8 * 2048], P.wall())
        P.op("sync", lambda q: q.nop(), waits=[("out", P.cnt["out"])] + P.wall())
        P.emit()
        return nc, din, dout, dbg
    ph5.close()
    ctx_es.close()
    sc1_es.close()
    tap("X2", X2[...], [128, 8 * 2048], P.wall())
    if upto <= 5:
        P.op("sync", lambda q: q.nop(), waits=[("out", P.cnt["out"])] + P.wall())
        P.emit()
        return nc, din, dout, dbg

    ph6 = ExitStack()
    XN = P.alloc("XN", [128, 8 * 2048], F32, stack=ph6, side="right")
    xnt = lambda tb: XN[:, tb * 2048:(tb + 1) * 2048]
    SC2x = [P.alloc(f"SC2x{i}", [128, 2048], F32, stack=ph6, side="right")
            for i in range(4)]
    TSall = P.alloc("TSall", [128, 512], F32, stack=ph6, side="right")
    K64a = P.alloc("K64a", [128, 512], F32, stack=ph6, side="right")
    CS64 = P.alloc("CS64", [128, 64], F32, stack=ph6, side="right")
    CM8 = P.alloc("CM8", [128, 8], F32, stack=ph6, side="right")
    ps_tr = P.psum("ps_tr", [64, 128], stack=ph6)

    ts_done = {}
    gam_done = {}
    for h in range(2):
        sl = slice(h * 4, (h + 1) * 4)
        for tb in range(h * 4, h * 4 + 4):
            bn1_tb(x2t, tb, [("gp", res_gp[tb])])
        agg_half(h)
        P.op("vector", lambda q, h=h, sl=sl: q.tensor_copy(
            STK[:, h * 4:h * 4 + 4], MU8[:, sl]), drain=False)
        P.op("vector", lambda q, h=h, sl=sl: q.tensor_copy(
            STK[:, 8 + h * 4:8 + h * 4 + 4], R8[:, sl]), drain=False)
        P.op("vector", lambda q, h=h, sl=sl: q.tensor_copy(
            STK[:, 16 + h * 4:16 + h * 4 + 4], V8[:, sl]), drain=False)
        RHO = M8["RHO8"][...]
        P.op("vector", lambda q, sl=sl: q.tensor_tensor(
            SA8[:, sl], R8[:, sl], R8[:, sl], OP.mult))
        P.op("vector", lambda q, sl=sl: q.tensor_tensor(
            SA8[:, sl], SA8[:, sl], V8[:, sl], OP.mult), inc="dve")
        va = P.op("scalar", lambda q, sl=sl: q.activation(
            out=SA8[:, sl], in_=SA8[:, sl], func=AF.Sqrt,
            bias=sb["epsc"][:, :]),
            waits=[("dve", P.cnt["dve"])], inc="act")
        P.op("vector", lambda q, sl=sl: q.reciprocal(SA8[:, sl], SA8[:, sl]),
             waits=[("act", va)])
        P.op("vector", lambda q, sl=sl, RHO=RHO: q.tensor_tensor(
            RHO[:, sl], R8[:, sl], SA8[:, sl], OP.mult))
        P.op("vector", lambda q, sl=sl, RHO=RHO: q.tensor_tensor(
            BA8[:, sl], MU8[:, sl], RHO[:, sl], OP.mult))
        bv6h = P.op("vector", lambda q, sl=sl: q.tensor_single_scalar(
            BA8[:, sl], BA8[:, sl], -1.0, OP.mult), inc="dve")

        tbs = list(range(h * 4, h * 4 + 4))
        zs, gams, bets, sss, sgs = {}, {}, {}, {}, {}
        for tb in tbs:
            zs[tb] = P.op("scalar", lambda q, tb=tb: q.activation(
                out=xnt(tb), in_=x2t(tb), func=AF.Identity,
                scale=M8["RHO8"][:, tb:tb + 1], bias=M8["BA8"][:, tb:tb + 1]),
                waits=[("dve", bv6h)], inc="act", drain=False)
        for tb in tbs:
            gams[tb] = P.op("vector", lambda q, tb=tb: q.tensor_tensor(
                xnt(tb), xnt(tb), gB[...], OP.mult),
                waits=[("act", zs[tb])] +
                ([("ls", bc_last_ls)] if tb == 0 else []),
                inc="dve", drain=False)
        for tb in tbs:
            bets[tb] = P.op("gpsimd", lambda q, tb=tb: q.tensor_add(
                xnt(tb), xnt(tb), bB[...]), waits=[("dve", gams[tb])], inc="gp")
        for tb in tbs:
            sss[tb] = P.op("vector", lambda q, tb=tb: q.tensor_mul(
                SC2x[tb % 4][...], xnt(tb), ssB[...]),
                waits=[("gp", bets[tb])] +
                ([("dve", ts_done[tb - 4])] if tb >= 4 else []),
                inc="dve", drain=False)
        for tb in tbs:
            sgs[tb] = P.op("scalar", lambda q, tb=tb: q.activation(
                out=SC2x[tb % 4][...], in_=SC2x[tb % 4][...], func=AF.Sign),
                waits=[("dve", sss[tb])], inc="act", drain=False)
        for tb in tbs:
            ts_done[tb] = P.op("vector", lambda q, tb=tb: q.tensor_reduce(
                TSall[:, tb * 64:(tb + 1) * 64],
                SC2x[tb % 4][...].rearrange("p (i j) -> p i j", i=64),
                AX.X, OP.add),
                waits=[("act", sgs[tb])], inc="dve")

    P.op("vector", lambda q: q.tensor_reduce(
        CS64[...], TSall[...].rearrange("p (t c i) -> p t c i", t=8, c=8),
        AX.X, OP.add), waits=[("dve", ts_done[7])])
    P.op("vector", lambda q: q.tensor_single_scalar(CS64[...], CS64[...], 8.0,
                                                    OP.mult))
    P.op("vector", lambda q: q.tensor_tensor(
        CS64[...].rearrange("p (t c) -> p t c", t=8),
        CS64[...].rearrange("p (t c) -> p t c", t=8),
        C("iota8").unsqueeze(1).broadcast_to([128, 8, 8]), OP.subtract))
    P.op("vector", lambda q: q.tensor_reduce(
        CM8[...], CS64[...].rearrange("p (t c) -> p t c", t=8), AX.X, OP.max))
    P.op("vector", lambda q: q.tensor_tensor(
        CS64[...].rearrange("p (t c) -> p t c", t=8),
        CS64[...].rearrange("p (t c) -> p t c", t=8),
        CM8[...].unsqueeze(2).broadcast_to([128, 8, 8]), OP.is_equal))
    P.op("vector", lambda q: q.tensor_single_scalar(K64a[...], TSall[...], 64.0,
                                                    OP.mult))
    P.op("vector", lambda q: q.scalar_tensor_tensor(
        K64a[...].rearrange("p (t c i) -> p t c i", t=8, c=8),
        CS64[...].rearrange("p (t c) -> p t c", t=8).unsqueeze(3)
        .broadcast_to([128, 8, 8, 8]),
        16384.0,
        K64a[...].rearrange("p (t c i) -> p t c i", t=8, c=8), OP.mult, OP.add))
    P.op("vector", lambda q: q.tensor_tensor(
        K64a[...].rearrange("p (t i) -> p t i", t=8),
        K64a[...].rearrange("p (t i) -> p t i", t=8),
        C("iota64").unsqueeze(1).broadcast_to([128, 8, 64]), OP.subtract))
    P.op("vector", lambda q: q.tensor_reduce(
        SA8, K64a[...].rearrange("p (t i) -> p t i", t=8), AX.X, OP.max))
    tov = P.op("vector", lambda q: q.tensor_tensor(
        TOA[...].rearrange("p (t i) -> p t i", t=8),
        K64a[...].rearrange("p (t i) -> p t i", t=8),
        SA8.unsqueeze(2).broadcast_to([128, 8, 64]), OP.is_equal), inc="dve")

    prev_tr = 0
    for tb in range(8):
        trv = P.op("tensor", lambda q, tb=tb: q.transpose(
            ps_tr[...], TOA[:, tb * 64:(tb + 1) * 64], C("ident_f32")),
            waits=[("dve", tov), ("dve", prev_tr)], inc="pe")
        prev_tr = P.op("vector", lambda q, tb=tb: q.tensor_copy(
            OHT[:, tb * 128:(tb + 1) * 128], ps_tr[...]),
            waits=[("pe", trv)], inc="dve", drain=False)

    xsl_done = {}
    for hh in range(2):
        for tb in range(hh * 4, hh * 4 + 4):
            w = [("dve", tov)]
            if tb >= 4:
                w.append(("dve", xsl_done[tb - 4]))
            P.op("vector", lambda q, tb=tb: q.tensor_tensor(
                SC2x[tb % 4][...].rearrange("p (i j) -> p i j", i=64),
                xnt(tb).rearrange("p (i j) -> p i j", i=64),
                TOA[:, tb * 64:(tb + 1) * 64].unsqueeze(2)
                .broadcast_to([128, 64, DS]),
                OP.mult), waits=w)
        for tb in range(hh * 4, hh * 4 + 4):
            xsl_done[tb] = P.op("vector", lambda q, tb=tb: q.tensor_reduce(
                XSL[:, tb * DS:(tb + 1) * DS],
                SC2x[tb % 4][...].rearrange("p (i j) -> p j i", i=64),
                AX.X, OP.add),
                inc="dve")

    P.op("vector", lambda q: q.tensor_reduce(
        M8["TKN8"][...], XSL[...].rearrange("p (t j) -> p t j", t=8), AX.X, OP.min),
        waits=[("dve", xsl_done[7])])
    P.op("vector", lambda q: q.tensor_reduce(
        M8["TKX8"][...], XSL[...].rearrange("p (t j) -> p t j", t=8), AX.X, OP.max))
    P.op("vector", lambda q: q.tensor_tensor(
        K64a[...].rearrange("p (t i) -> p t i", t=8),
        TOA[...].rearrange("p (t i) -> p t i", t=8),
        M8["TKN8"][...].unsqueeze(2).broadcast_to([128, 8, 64]), OP.mult))
    P.op("vector", lambda q: q.tensor_scalar(
        TSall[...], TOA[...], -3e38, 3e38, OP.mult, OP.add))
    P.op("vector", lambda q: q.tensor_tensor(K64a[...], K64a[...], TSall[...],
                                             OP.add))
    P.op("vector", lambda q: q.tensor_reduce(
        TMIN[...], K64a[...].rearrange("p (t i) -> p i t", t=8), AX.X, OP.min))
    P.op("vector", lambda q: q.tensor_tensor(
        K64a[...].rearrange("p (t i) -> p t i", t=8),
        TOA[...].rearrange("p (t i) -> p t i", t=8),
        M8["TKX8"][...].unsqueeze(2).broadcast_to([128, 8, 64]), OP.mult))
    P.op("vector", lambda q: q.tensor_scalar(
        TSall[...], TOA[...], 3e38, -3e38, OP.mult, OP.add))
    P.op("vector", lambda q: q.tensor_tensor(K64a[...], K64a[...], TSall[...],
                                             OP.add))
    xs_done = P.op("vector", lambda q: q.tensor_reduce(
        TMAX[...], K64a[...].rearrange("p (t i) -> p i t", t=8), AX.X, OP.max),
        inc="dve")

    trv = P.op("tensor", lambda q: q.transpose(ps_tr[...], TMAX[...], C("ident_f32")),
               waits=[("dve", xs_done), ("dve", prev_tr)], inc="pe")
    mv = P.op("vector", lambda q: q.tensor_reduce(SEND[0:64, 0:1], ps_tr[...],
                                                  AX.X, OP.max),
              waits=[("pe", trv)], inc="dve")
    trv2 = P.op("tensor", lambda q: q.transpose(ps_tr[...], TMIN[...], C("ident_f32")),
                waits=[("dve", mv)], inc="pe")
    mv2 = P.op("vector", lambda q: q.tensor_reduce(SEND[64:128, 0:1], ps_tr[...],
                                                   AX.X, OP.min, negate=True),
               waits=[("pe", trv2)], inc="dve")

    P.dma("sync", dout["o_partials"], SEND[...], "out", waits=[("dve", mv2)])
    P.dma("sync", dout["o_xsl"], XSL[...], "out",
          waits=[("dve", xsl_done[7])])
    P.dma("sync", dout["o_to"], TOA[...], "out", waits=[("dve", tov)])
    P.dma("sync", dout["o_onehotT"], OHT[...], "out",
          waits=[("dve", prev_tr)])
    P.dma("sync", dout["o_stats"], STK[...], "out",
          waits=[("dve", ts_done[7])])
    P.op("sync", lambda q: q.nop(), waits=[("out", P.cnt["out"])])

    P.emit()
    return nc, din, dout, dbg


def build_launch2(debug=()):
    nc = bass.Bass("TRN2", target_bir_lowering=False, detect_race_conditions=False)
    din = {nm: nc.dram_tensor(nm, shape, dt, kind="ExternalInput").ap()
           for nm, shape, dt in IN2_SPECS}
    dout = nc.dram_tensor("o_out", [TQ, D], F32, kind="ExternalOutput").ap()
    dbg = {}

    P = Prog(nc)
    sb = P.sb
    epsc = P.alloc("epsc", [128, 1], F32)
    P.op("vector", lambda q: q.memset(epsc[:, :], LNEPS), inc="dve")
    for nm, shape, dt in CONSTS2:
        P.alloc("c_" + nm, shape, dt)
    C = lambda nm: sb["c_" + nm][...]

    X2 = P.alloc("X2", [128, 8 * 2048], F32)
    x2t = lambda tb: X2[:, tb * 2048:(tb + 1) * 2048]
    XSL = P.alloc("XSL", [128, 8 * DS], F32)
    TOA = P.alloc("TOA", [128, 8 * 64], F32)
    OHT = P.alloc("OHT", [64, 8 * 128], F32)
    STK = P.alloc("STK", [128, 24], F32)
    PART = P.alloc("PART", [128, NC], F32)
    GM = P.alloc("GM", [128, 1], F32)
    TMINC = P.alloc("TMINC", [64, 1], F32)
    M8 = {nm: P.alloc(nm, [128, 8], F32)
          for nm in ("MN8", "MX8", "SC8", "INV8", "R2P8", "VY8", "MUF8", "RF8",
                     "CR8", "DD8", "SCL8", "BO8", "T8")}
    SEL = P.alloc("SEL", [128, 32], F32)
    U256 = P.alloc("U256", [128, 8 * DS], F32)
    GS256 = P.alloc("GS256", [128, 8 * DS], F32)
    XL256 = P.alloc("XL256", [128, 8 * DS], F32)
    B256 = P.alloc("B256", [128, 8 * DS], F32)
    S256 = P.alloc("S256", [128, 8 * DS], F32)
    DEL = P.alloc("DEL", [128, 8 * DS], F32)
    Y2S = P.alloc("Y2S", [128, 8 * DS], F32)
    GSEL = P.alloc("GSEL", [128, 8 * DS], F32)
    CMPX = P.alloc("CMPX", [128, 8 * DS * GRID], F32)
    INDX = P.alloc("INDX", [128, 8 * DS * GRID], F32)
    TBL = P.alloc("TBL", [128, 8 * DS * GRID], F32)
    TBL2 = P.alloc("TBL2", [128, 8 * DS * GRID], F32)
    SC2x = [P.alloc(f"SC2x{i}", [128, 2048], F32) for i in range(2)]
    OUTT = [P.alloc(f"OUTT{i}", [128, 2048], F32) for i in range(2)]

    ps_m = P.psum("ps_m", [128, 32])
    ps_g = P.psum("ps_g", [128, 512])
    ps_tb = [P.psum(f"ps_tb{i}", [128, 512]) for i in range(2)]

    for nm, shape, dt in CONSTS2:
        P.dma("sync", C(nm), din[nm], "ld")
    P.dma("sync", XSL[...], din["xsl"], "ld")
    P.dma("sync", TOA[...], din["to_all"], "ld")
    P.dma("sync", OHT[...], din["onehotT"], "ld")
    P.dma("sync", STK[...], din["stats"], "ld")
    P.dma("sync", PART[...], din["partials"], "ld")
    ld_all = P.cnt["ld"]
    x2_ld = []
    for tb in range(8):
        x2_ld.append(P.dma(
            "sync", x2t(tb).rearrange("p (o d) -> p o d", o=1),
            din["x2"].rearrange("(a p) d -> p a d", p=128)[:, tb:tb + 1, :],
            "ld"))
    mu2 = STK[:, 0:8]
    r2 = STK[:, 8:16]
    v2 = STK[:, 16:24]

    P.op("vector", lambda q: q.tensor_reduce(GM[...], PART[...], AX.X, OP.max),
         waits=[("ld", ld_all)])
    gv = P.op("vector", lambda q: q.tensor_single_scalar(
        TMINC[...], GM[64:128, 0:1], -1.0, OP.mult), inc="dve")

    sel_pe = []
    for tb in range(8):
        oht = OHT[:, tb * 128:(tb + 1) * 128]
        for i, rhs in enumerate([TMINC[...], GM[0:64, 0:1], C("oscale")]):
            v = P.op("tensor", lambda q, oht=oht, rhs=rhs, tb=tb, i=i: q.matmul(
                ps_m[:, tb * 3 + i: tb * 3 + i + 1], oht, rhs,
                start=True, stop=True),
                waits=[("dve", gv)] if (tb == 0 and i == 0) else (), inc="pe")
    sel_pe = P.cnt["pe"]
    e1 = P.op("vector", lambda q: q.tensor_copy(SEL[:, 0:24], ps_m[:, 0:24]),
              waits=[("pe", sel_pe)], inc="dve")
    mn3 = SEL[:, 0:24].rearrange("p (t k) -> p t k", t=8)
    P.op("vector", lambda q: q.tensor_tensor(
        M8["MN8"][...], mn3[:, :, 0:1].squeeze(), mn3[:, :, 0:1].squeeze(), OP.max),
        waits=[("dve", e1)])
    P.op("vector", lambda q: q.tensor_tensor(
        M8["MX8"][...], mn3[:, :, 1:2].squeeze(), mn3[:, :, 1:2].squeeze(), OP.max))
    P.op("vector", lambda q: q.tensor_tensor(
        M8["SC8"][...], mn3[:, :, 2:3].squeeze(), mn3[:, :, 2:3].squeeze(), OP.max))
    for tb in range(8):
        oht = OHT[:, tb * 128:(tb + 1) * 128]
        P.op("tensor", lambda q, oht=oht, tb=tb: q.matmul(
            ps_g[:, tb * DS:(tb + 1) * DS], oht, C("gam_rs"),
            start=True, stop=True), inc="pe")
    gp_pe = P.cnt["pe"]
    e2 = P.op("vector", lambda q: q.tensor_copy(GSEL[...], ps_g[:, 0:256]),
              waits=[("pe", gp_pe)], inc="dve")
    for tb in range(8):
        oht = OHT[:, tb * 128:(tb + 1) * 128]
        P.op("tensor", lambda q, oht=oht, tb=tb: q.matmul(
            ps_g[:, tb * DS:(tb + 1) * DS], oht, C("bet_rs"),
            start=True, stop=True), waits=[("dve", e2)] if tb == 0 else (),
            inc="pe")
    bp_pe = P.cnt["pe"]
    P.op("vector", lambda q: q.tensor_tensor(Y2S[...], XSL[...], ps_g[:, 0:256],
                                             OP.subtract),
         waits=[("pe", bp_pe)], inc="dve")
    tbl_evac = {}
    for tb in range(8):
        oht = OHT[:, tb * 128:(tb + 1) * 128]
        v = P.op("tensor", lambda q, oht=oht, tb=tb: q.matmul(
            ps_tb[tb % 2][...], oht, C("bases"), start=True, stop=True),
            waits=[("dve", tbl_evac[tb - 2])] if tb >= 2 else (), inc="pe")
        tbl_evac[tb] = P.op("vector", lambda q, tb=tb: q.tensor_copy(
            TBL[:, tb * 512:(tb + 1) * 512], ps_tb[tb % 2][...]),
            waits=[("pe", v)], inc="dve", drain=False)
    bases_done = P.cnt["dve"]

    P.op("vector", lambda q: q.tensor_tensor(M8["INV8"][...], M8["MX8"][...],
                                             M8["MN8"][...], OP.subtract))
    P.op("vector", lambda q: q.tensor_single_scalar(
        M8["INV8"][...], M8["INV8"][...], 1e-8, OP.add))
    P.op("vector", lambda q: q.reciprocal(M8["INV8"][...], M8["INV8"][...]))
    P.op("vector", lambda q: q.tensor_tensor(
        U256[...].rearrange("p (t j) -> p t j", t=8),
        XSL[...].rearrange("p (t j) -> p t j", t=8),
        M8["MN8"][...].unsqueeze(2).broadcast_to([128, 8, DS]), OP.subtract))
    P.op("vector", lambda q: q.tensor_tensor(
        U256[...].rearrange("p (t j) -> p t j", t=8),
        U256[...].rearrange("p (t j) -> p t j", t=8),
        M8["INV8"][...].unsqueeze(2).broadcast_to([128, 8, DS]), OP.mult))
    P.op("vector", lambda q: q.tensor_scalar(
        U256[...], U256[...], 0.0, CLIPHI, OP.max, OP.min))
    P.op("vector", lambda q: q.tensor_tensor(
        CMPX[...].rearrange("p (t j g) -> p t j g", t=8, j=DS),
        U256[...].rearrange("p (t j) -> p t j", t=8).unsqueeze(3)
        .broadcast_to([128, 8, DS, GRID]),
        C("grid_cmp").rearrange("p (j g) -> p j g", j=DS).unsqueeze(1)
        .broadcast_to([128, 8, DS, GRID]), OP.is_ge))
    P.op("vector", lambda q: q.tensor_reduce(
        GS256[...], CMPX[...].rearrange("p (t j g) -> p (t j) g", t=8, j=DS),
        AX.X, OP.add))
    P.op("vector", lambda q: q.tensor_single_scalar(XL256[...], U256[...], 16.0,
                                                    OP.mult))
    P.op("vector", lambda q: q.tensor_tensor(XL256[...], XL256[...], GS256[...],
                                             OP.subtract))
    P.op("vector", lambda q: q.tensor_single_scalar(XL256[...], XL256[...], 1.0,
                                                    OP.add))
    cmp4 = CMPX[...].rearrange("p (x g) -> p x g", g=GRID)
    ind4 = INDX[...].rearrange("p (x g) -> p x g", g=GRID)
    P.op("vector", lambda q: q.tensor_tensor(
        ind4[:, :, 0:GRID - 1], cmp4[:, :, 0:GRID - 1], cmp4[:, :, 1:GRID],
        OP.subtract))
    iv = P.op("vector", lambda q: q.tensor_tensor(
        ind4[:, :, GRID - 1:GRID], cmp4[:, :, GRID - 1:GRID],
        cmp4[:, :, GRID - 1:GRID], OP.max), inc="dve")
    bmul = P.op("gpsimd", lambda q: q.tensor_mul(CMPX[...], INDX[...],
                                                 TBL[...]),
                waits=[("dve", bases_done), ("dve", iv)], inc="gp")
    bsel_done = P.op("vector", lambda q: q.tensor_reduce(
        B256[...], CMPX[...].rearrange("p (x g) -> p x g", g=GRID), AX.X, OP.add),
        waits=[("gp", bmul)], inc="dve")
    tbl_evac = {}
    for tb in range(8):
        oht = OHT[:, tb * 128:(tb + 1) * 128]
        w = [("dve", tbl_evac[tb - 2])] if tb >= 2 else []
        v = P.op("tensor", lambda q, oht=oht, tb=tb: q.matmul(
            ps_tb[tb % 2][...], oht, C("slopes"), start=True, stop=True),
            waits=w, inc="pe")
        tbl_evac[tb] = P.op("vector", lambda q, tb=tb: q.tensor_copy(
            TBL2[:, tb * 512:(tb + 1) * 512], ps_tb[tb % 2][...]),
            waits=[("pe", v)], inc="dve", drain=False)
    slopes_done = P.cnt["dve"]
    smul = P.op("gpsimd", lambda q: q.tensor_mul(INDX[...], INDX[...],
                                                 TBL2[...]),
                waits=[("dve", slopes_done), ("gp", bmul)], inc="gp")
    P.op("vector", lambda q: q.tensor_reduce(
        S256[...], INDX[...].rearrange("p (x g) -> p x g", g=GRID), AX.X, OP.add),
         waits=[("gp", smul)])
    P.op("vector", lambda q: q.tensor_tensor(DEL[...], S256[...], XL256[...],
                                             OP.mult))
    P.op("vector", lambda q: q.tensor_tensor(DEL[...], DEL[...], B256[...], OP.add))
    P.op("vector", lambda q: q.tensor_tensor(
        DEL[...].rearrange("p (t j) -> p t j", t=8),
        DEL[...].rearrange("p (t j) -> p t j", t=8),
        M8["SC8"][...].unsqueeze(2).broadcast_to([128, 8, DS]), OP.mult))
    P.op("vector", lambda q: q.tensor_tensor(M8["VY8"][...], r2, r2, OP.mult))
    P.op("vector", lambda q: q.tensor_tensor(M8["VY8"][...], M8["VY8"][...], v2,
                                             OP.mult), inc="dve")
    a1 = P.op("scalar", lambda q: q.activation(
        out=M8["R2P8"][...], in_=M8["VY8"][...], func=AF.Sqrt,
        bias=epsc[:, :]), waits=[("dve", P.cnt["dve"])], inc="act")
    P.op("vector", lambda q: q.reciprocal(M8["R2P8"][...], M8["R2P8"][...]),
         waits=[("act", a1)])
    P.op("vector", lambda q: q.reciprocal(GSEL[...], GSEL[...]))
    P.op("vector", lambda q: q.tensor_tensor(Y2S[...], Y2S[...], GSEL[...],
                                             OP.mult))
    P.op("vector", lambda q: q.tensor_tensor(
        Y2S[...].rearrange("p (t j) -> p t j", t=8),
        Y2S[...].rearrange("p (t j) -> p t j", t=8),
        M8["R2P8"][...].unsqueeze(2).broadcast_to([128, 8, DS]), OP.mult))
    P.op("vector", lambda q: q.tensor_reduce(
        M8["MUF8"][...], DEL[...].rearrange("p (t j) -> p t j", t=8), AX.X, OP.add))
    P.op("vector", lambda q: q.tensor_single_scalar(
        M8["MUF8"][...], M8["MUF8"][...], 1.0 / 2048.0, OP.mult))
    P.op("vector", lambda q: q.tensor_tensor(GS256[...], Y2S[...], DEL[...],
                                             OP.mult))
    P.op("vector", lambda q: q.tensor_reduce(
        M8["CR8"][...], GS256[...].rearrange("p (t j) -> p t j", t=8), AX.X, OP.add))
    P.op("vector", lambda q: q.tensor_tensor(GS256[...], DEL[...], DEL[...],
                                             OP.mult))
    P.op("vector", lambda q: q.tensor_reduce(
        M8["DD8"][...], GS256[...].rearrange("p (t j) -> p t j", t=8), AX.X, OP.add))
    P.op("vector", lambda q: q.scalar_tensor_tensor(
        M8["DD8"][...], M8["CR8"][...], 2.0, M8["DD8"][...], OP.mult, OP.add))
    P.op("vector", lambda q: q.tensor_single_scalar(
        M8["DD8"][...], M8["DD8"][...], 1.0 / 2048.0, OP.mult))
    P.op("vector", lambda q: q.tensor_tensor(M8["DD8"][...], M8["DD8"][...],
                                             M8["VY8"][...], OP.add))
    P.op("vector", lambda q: q.tensor_tensor(M8["T8"][...], M8["MUF8"][...],
                                             M8["MUF8"][...], OP.mult))
    P.op("vector", lambda q: q.tensor_tensor(M8["DD8"][...], M8["DD8"][...],
                                             M8["T8"][...], OP.subtract), inc="dve")
    a2 = P.op("scalar", lambda q: q.activation(
        out=M8["RF8"][...], in_=M8["DD8"][...], func=AF.Sqrt, bias=epsc[:, :]),
        waits=[("dve", P.cnt["dve"])], inc="act")
    P.op("vector", lambda q: q.reciprocal(M8["RF8"][...], M8["RF8"][...]),
         waits=[("act", a2)])
    P.op("vector", lambda q: q.tensor_tensor(M8["SCL8"][...], M8["RF8"][...], r2,
                                             OP.mult))
    P.op("vector", lambda q: q.tensor_single_scalar(
        M8["SCL8"][...], M8["SCL8"][...], 1.0, OP.add))
    P.op("vector", lambda q: q.tensor_tensor(M8["BO8"][...], mu2, r2, OP.mult))
    P.op("vector", lambda q: q.tensor_tensor(M8["BO8"][...], M8["BO8"][...],
                                             M8["MUF8"][...], OP.add))
    P.op("vector", lambda q: q.tensor_tensor(M8["BO8"][...], M8["BO8"][...],
                                             M8["RF8"][...], OP.mult))
    P.op("vector", lambda q: q.tensor_single_scalar(
        M8["BO8"][...], M8["BO8"][...], -1.0, OP.mult))
    delrf = P.op("vector", lambda q: q.tensor_tensor(
        DEL[...].rearrange("p (t j) -> p t j", t=8),
        DEL[...].rearrange("p (t j) -> p t j", t=8),
        M8["RF8"][...].unsqueeze(2).broadcast_to([128, 8, DS]), OP.mult),
        inc="dve")

    out_dma = []
    ds_done = {}
    for tb in range(8):
        wa = [("dve", delrf), ("ld", x2_ld[tb])]
        if tb >= 2:
            wa.append(("o%d" % (tb % 2), out_dma[tb - 2]))
        av = P.op("scalar", lambda q, tb=tb: q.activation(
            out=OUTT[tb % 2][...], in_=x2t(tb), func=AF.Identity,
            scale=M8["SCL8"][:, tb:tb + 1], bias=M8["BO8"][:, tb:tb + 1]),
            waits=wa, inc="act", drain=False)
        wd = [("dve", delrf)]
        if tb >= 2:
            wd.append(("gp", ds_done[tb - 2]))
        dsv = P.op("vector", lambda q, tb=tb: q.tensor_tensor(
            SC2x[tb % 2][...].rearrange("p (i j) -> p i j", i=64),
            TOA[:, tb * 64:(tb + 1) * 64].unsqueeze(2).broadcast_to([128, 64, DS]),
            DEL[:, tb * DS:(tb + 1) * DS].unsqueeze(1).broadcast_to([128, 64, DS]),
            OP.mult), waits=wd, inc="dve", drain=False)
        ds_done[tb] = P.op("gpsimd", lambda q, tb=tb: q.tensor_add(
            OUTT[tb % 2][...], OUTT[tb % 2][...], SC2x[tb % 2][...]),
            waits=[("act", av), ("dve", dsv)], inc="gp")
        out_dma.append(P.dma("sync", dout[tb * 128:(tb + 1) * 128, :],
                             OUTT[tb % 2][...], "o%d" % (tb % 2),
                             waits=[("gp", ds_done[tb])]))
    P.op("sync", lambda q: q.nop(),
         waits=[("o0", P.cnt["o0"]), ("o1", P.cnt["o1"])])
    P.emit()
    return nc, din, dout, dbg


def _host_prep(inputs):
    x = np.asarray(inputs["L0_attn_x"], np.float32)
    Wq = np.asarray(inputs["L0_attn_Wq"], np.float32)
    Wk = np.asarray(inputs["L0_attn_Wk"], np.float32)
    Wv = np.asarray(inputs["L0_attn_Wv"], np.float32)
    Wo = np.asarray(inputs["L0_attn_Wo"], np.float32)
    bo = np.asarray(inputs["L0_attn_bo"], np.float32)
    gam = np.asarray(inputs["L0_ffn_ln_gamma"], np.float32)
    bet = np.asarray(inputs["L0_ffn_ln_beta"], np.float32)
    bas = np.asarray(inputs["L0_ffn_spline_bases"], np.float32)
    slo = np.asarray(inputs["L0_ffn_spline_slopes"], np.float32)
    osc = np.asarray(inputs["L0_ffn_output_scale"], np.float32)

    Wq_t = np.ascontiguousarray(
        Wq.reshape(16, 128, 16, 128).transpose(2, 1, 0, 3).reshape(16, 128, 2048))
    Wk_t = np.ascontiguousarray(
        Wk.reshape(16, 128, 4, 128).transpose(2, 1, 0, 3).reshape(4, 128, 2048))

    qoff = np.arange(128)[None, :]
    koff = np.arange(128)[:, None]
    tri = np.where(qoff >= koff, 0.0, NEG).astype(bfdt)
    common1 = {
        "Wq_t": Wq_t, "Wk_t": Wk_t, "Wv": Wv, "Wo": Wo,
        "ident_bf": np.eye(128, dtype=bfdt).view(np.uint16),
        "ident_f32": np.eye(128, dtype=np.float32),
        "one11": np.ones((1, 1), np.float32),
        "tri": tri.view(np.uint16),
        "ones_col_bf": np.ones((128, 1), bfdt).view(np.uint16),
        "ones_row_f32": np.ones((1, 128), np.float32),
        "iota64": np.tile(np.arange(64, dtype=np.float32)[None, :], (128, 1)),
        "iota8": np.tile(np.arange(8, dtype=np.float32)[None, :], (128, 1)),
        "bo": np.ascontiguousarray(bo.reshape(1, D)),
        "gam": np.ascontiguousarray(gam.reshape(1, D)),
        "bet": np.ascontiguousarray(bet.reshape(1, D)),
        "slopes": np.ascontiguousarray(slo.reshape(NT, DS * GRID)),
    }
    common2 = {
        "ident_f32": common1["ident_f32"],
        "grid_cmp": np.tile((np.arange(GRID, dtype=np.float32) / GRID)[None, :],
                            (128, DS)).reshape(128, DS * GRID),
        "bases": np.ascontiguousarray(bas.reshape(NT, DS * GRID)),
        "slopes": common1["slopes"],
        "oscale": np.ascontiguousarray(osc.reshape(NT, 1)),
        "gam_rs": np.ascontiguousarray(gam.reshape(NT, DS)),
        "bet_rs": np.ascontiguousarray(bet.reshape(NT, DS)),
    }

    in1, in2base = [], []
    for c in range(NC):
        b, par = c // 2, c % 2
        xl = x[b].reshape(16, 128, D)
        if par:
            xl = xl[[i ^ 1 for i in range(16)]]
        xl = np.ascontiguousarray(xl.reshape(T, D))
        m = dict(common1)
        m["xT"] = np.ascontiguousarray(xl.T)
        m["xq"] = np.ascontiguousarray(xl.reshape(16, 128, D)[0::2].reshape(TQ, D))
        m["parity_mask"] = np.full((128, 128), 0.0 if par else NEG,
                                   dtype=bfdt).view(np.uint16)
        bm = np.zeros((8, 4, 128, 128), np.float32)
        for kbp in range(8):
            for j in range(4):
                if kbp < 2 * j:
                    bm[kbp, j] = 1.0
                elif kbp == 2 * j:
                    bm[kbp, j] = (qoff >= koff).astype(np.float32)
                elif kbp == 2 * j + 1:
                    bm[kbp, j] = 1.0 if par else 0.0
        m["bmask"] = np.ascontiguousarray(
            bm.transpose(2, 0, 1, 3).reshape(128, 8 * 512).astype(bfdt)
        ).view(np.uint16)
        in1.append(m)
        in2base.append(dict(common2))
    return in1, in2base


def kernel(**inputs):
    in1, in2base = _host_prep(inputs)
    nc1, _, _, _ = build_launch1()
    r1 = run_bass_kernel_spmd(nc1, in1, core_ids=list(range(NC))).results

    partials = np.stack([np.asarray(r1[c]["o_partials"])[:, 0] for c in range(NC)],
                        axis=1).astype(np.float32)
    in2 = []
    for c in range(NC):
        m = dict(in2base[c])
        m["x2"] = np.ascontiguousarray(np.asarray(r1[c]["o_x2"], np.float32))
        m["partials"] = np.ascontiguousarray(partials)
        m["xsl"] = np.ascontiguousarray(np.asarray(r1[c]["o_xsl"], np.float32))
        m["to_all"] = np.ascontiguousarray(np.asarray(r1[c]["o_to"], np.float32))
        m["onehotT"] = np.ascontiguousarray(np.asarray(r1[c]["o_onehotT"], np.float32))
        m["stats"] = np.ascontiguousarray(np.asarray(r1[c]["o_stats"], np.float32))
        in2.append(m)
    nc2, _, _, _ = build_launch2()
    r2 = run_bass_kernel_spmd(nc2, in2, core_ids=list(range(NC))).results

    out = np.empty((B, T, D), np.float32)
    for c in range(NC):
        b, par = c // 2, c % 2
        o = np.asarray(r2[c]["o_out"], np.float32).reshape(8, 128, D)
        for s in range(8):
            out[b, (2 * s + par) * 128:(2 * s + par + 1) * 128, :] = o[s]
    return out


# version 13
# speedup vs baseline: 1.0104x; 1.0104x over previous
import numpy as np
import ml_dtypes
from contextlib import ExitStack

import concourse.bass as bass
import concourse.mybir as mybir
from concourse.alu_op_type import AluOpType
from concourse.bass_utils import run_bass_kernel_spmd

F32 = mybir.dt.float32
BF16 = mybir.dt.bfloat16
U16 = mybir.dt.uint16
BF16_CONSTS = {"ident_bf", "tri", "parity_mask", "ones_col_bf"}
AF = mybir.ActivationFunctionType
AX = mybir.AxisListType
OP = AluOpType

B, T, D = 4, 2048, 2048
G, PHD, DH = 4, 4, 128
KV = G * DH
NC = 8
TQ = 1024
NT, DS, GRID = 64, 32, 16
LNEPS = 1e-5
ISQD = float(1.0 / np.float32(np.sqrt(np.float32(DH))))
NEG = -1e30
CLIPHI = float(np.float32(1.0) - np.float32(1e-6))

bfdt = ml_dtypes.bfloat16
ENGS = ("sync", "gpsimd", "tensor", "vector", "scalar")

CONSTS1 = [
    ("ident_bf", [128, 128], U16),
    ("tri", [128, 128], U16), ("parity_mask", [128, 128], U16),
    ("ones_col_bf", [128, 1], U16), ("ones_row_f32", [1, 128], F32),
    ("one11", [1, 1], F32), ("ident_f32", [128, 128], F32),
    ("iota64", [128, 64], F32), ("iota8", [128, 8], F32),
    ("slopes", [NT, DS * GRID], F32),
]
IN1_SPECS = [
    ("xT", [D, T], F32), ("xq", [TQ, D], F32),
    ("Wq_t", [16, 128, 2048], F32), ("Wk_t", [4, 128, 2048], F32),
    ("Wv", [D, KV], F32), ("Wo", [D, D], F32),
    ("gam", [1, D], F32), ("bet", [1, D], F32), ("bo", [1, D], F32),
    ("bmask", [128, 8 * 512], U16),
] + CONSTS1

CONSTS2 = [
    ("ident_f32", [128, 128], F32),
    ("grid_cmp", [128, DS * GRID], F32),
    ("bases", [NT, DS * GRID], F32), ("slopes", [NT, DS * GRID], F32),
    ("oscale", [NT, 1], F32),
    ("gam_rs", [NT, DS], F32), ("bet_rs", [NT, DS], F32),
]
IN2_SPECS = [
    ("x2", [TQ, D], F32), ("partials", [128, NC], F32),
    ("xsl", [128, 8 * DS], F32), ("to_all", [128, 8 * 64], F32),
    ("onehotT", [64, 8 * 128], F32), ("stats", [128, 24], F32),
] + CONSTS2


class Prog:

    def __init__(self, nc):
        self.nc = nc
        self.es = ExitStack()
        self.sem = {}
        self.cnt = {}
        for nm in ("ld", "lx", "ls", "gx", "gk", "q0", "q1", "wo", "out",
                   "o0", "o1", "pe", "act", "dve", "gp"):
            self.sem[nm] = self.es.enter_context(nc.semaphore(nm))
            self.cnt[nm] = 0
        self.ops = {e: [] for e in ENGS}
        self.sb = {}

    def op(self, eng, fn, waits=(), inc=None, inck=1, kind="compute", drain=True):
        w = [(nm, v) for nm, v in waits if v and v > 0]
        tag = None
        if inc is not None:
            self.cnt[inc] += inck
            tag = (inc, inck)
        self.ops[eng].append((fn, w, tag, kind if drain else "nodrain"))
        return self.cnt[inc] if inc else None

    def dma(self, eng, out, in_, sem, waits=()):
        return self.op(eng, lambda q: q.dma_start(out=out, in_=in_),
                       waits=waits, inc=sem, inck=16, kind="dma")

    def alloc(self, nm, shape, dt, stack=None, side=None):
        kw = {"side": side} if side else {}
        t = (stack or self.es).enter_context(self.nc.sbuf_tensor(nm, shape, dt, **kw))
        self.sb[nm] = t
        return t

    def psum(self, nm, shape, stack=None, dt=F32):
        return (stack or self.es).enter_context(self.nc.psum_tensor(nm, shape, dt))

    def wall(self, names=("ld", "lx", "ls", "gx", "gk", "q0", "q1", "wo",
                          "pe", "act", "dve", "gp")):
        return [(nm, self.cnt[nm]) for nm in names]

    def ln_stats(self, src_tile, scr, out_mu, out_r, out_var, waits=()):
        st24, s8 = scr["st24"], scr["s8"]
        sA, sB = scr["sA"], scr["sB"]
        for cb in range(4):
            self.op("vector", lambda q, cb=cb: q.bn_stats(
                st24[:, cb * 6:(cb + 1) * 6], src_tile[:, cb * 512:(cb + 1) * 512]),
                waits=waits if cb == 0 else ())
        tri = st24[:, :].rearrange("p (a b) -> p a b", a=8)
        mus = tri[:, :, 1:2].squeeze()
        m2s = tri[:, :, 2:3].squeeze()
        self.op("vector", lambda q: q.tensor_tensor(s8[...], mus, mus, OP.mult))
        self.op("vector", lambda q: q.tensor_reduce(sA[...], s8[...], AX.X, OP.add))
        self.op("vector", lambda q: q.tensor_reduce(sB[...], m2s, AX.X, OP.add))
        self.op("vector", lambda q: q.scalar_tensor_tensor(
            sB[...], sA[...], 256.0, sB[...], OP.mult, OP.add))
        self.op("vector", lambda q: q.tensor_reduce(out_mu, mus, AX.X, OP.add))
        self.op("vector", lambda q: q.tensor_single_scalar(
            out_mu, out_mu, 1.0 / 8.0, OP.mult))
        self.op("vector", lambda q: q.tensor_tensor(sA[...], out_mu, out_mu, OP.mult))
        self.op("vector", lambda q: q.tensor_single_scalar(
            out_var, sB[...], 1.0 / 2048.0, OP.mult))
        v = self.op("vector", lambda q: q.tensor_tensor(
            out_var, out_var, sA[...], OP.subtract), inc="dve")
        a = self.op("scalar", lambda q: q.activation(
            out=out_r, in_=out_var, func=AF.Sqrt, bias=scr["eps"][:, :]),
            waits=[("dve", v)], inc="act")
        r = self.op("vector", lambda q: q.reciprocal(out_r, out_r),
                    waits=[("act", a)], inc="dve")
        return r

    def ln_stats8(self, x2t, ST, S64, MU8, R8, V8, SA8, SB8, eps, waits=()):
        first = True
        for tb in range(8):
            for cb in range(4):
                self.op("vector", lambda q, tb=tb, cb=cb: q.bn_stats(
                    ST[:, tb * 24 + cb * 6: tb * 24 + (cb + 1) * 6],
                    x2t(tb)[:, cb * 512:(cb + 1) * 512]),
                    waits=waits if first else (), drain=False)
                first = False
        self.op("vector", lambda q: q.nop())
        v4 = ST[:, :].rearrange("p (t a b) -> p t a b", t=8, a=8)
        mus = v4[:, :, :, 1:2].squeeze()
        m2s = v4[:, :, :, 2:3].squeeze()
        self.op("vector", lambda q: q.tensor_tensor(S64[...], mus, mus, OP.mult))
        self.op("vector", lambda q: q.tensor_reduce(
            SA8[...], S64[...].rearrange("p (t a) -> p t a", t=8), AX.X, OP.add))
        self.op("vector", lambda q: q.tensor_reduce(SB8[...], m2s, AX.X, OP.add))
        self.op("vector", lambda q: q.scalar_tensor_tensor(
            SB8[...], SA8[...], 256.0, SB8[...], OP.mult, OP.add))
        self.op("vector", lambda q: q.tensor_reduce(MU8[...], mus, AX.X, OP.add))
        self.op("vector", lambda q: q.tensor_single_scalar(
            MU8[...], MU8[...], 1.0 / 8.0, OP.mult))
        self.op("vector", lambda q: q.tensor_tensor(SA8[...], MU8[...], MU8[...],
                                                    OP.mult))
        self.op("vector", lambda q: q.tensor_single_scalar(
            V8[...], SB8[...], 1.0 / 2048.0, OP.mult))
        v = self.op("vector", lambda q: q.tensor_tensor(
            V8[...], V8[...], SA8[...], OP.subtract), inc="dve")
        a = self.op("scalar", lambda q: q.activation(
            out=R8[...], in_=V8[...], func=AF.Sqrt, bias=eps[:, :]),
            waits=[("dve", v)], inc="act")
        r = self.op("vector", lambda q: q.reciprocal(R8[...], R8[...]),
                    waits=[("act", a)], inc="dve")
        return r

    def emit(self):
        with self.nc.Block() as blk:
            for eng in ENGS:
                seq = self.ops[eng]
                if not seq:
                    continue

                def section(q, seq=seq, eng=eng):
                    autodrain = eng in ("vector", "scalar")
                    for fn, waits, tag, kind in seq:
                        for nm, v in waits:
                            q.wait_ge(self.sem[nm], v)
                        instr = fn(q)
                        if autodrain and kind == "compute":
                            instr = q.drain()
                        if tag is not None:
                            instr.then_inc(self.sem[tag[0]], tag[1])

                getattr(blk, eng)(section)


def build_launch1(debug=(), upto=9):
    nc = bass.Bass("TRN2", target_bir_lowering=False, detect_race_conditions=False)
    din = {nm: nc.dram_tensor(nm, shape, dt, kind="ExternalInput").ap()
           for nm, shape, dt in IN1_SPECS}
    dout = {nm: nc.dram_tensor(nm, shape, F32, kind="ExternalOutput").ap()
            for nm, shape in [
                ("o_partials", [128, 1]), ("o_x2", [TQ, D]),
                ("o_xsl", [128, 8 * DS]), ("o_to", [128, 8 * 64]),
                ("o_onehotT", [64, 8 * 128]), ("o_stats", [128, 24]),
            ]}
    dbg = {}
    ssrow_dram = nc.dram_tensor("ssrow_dram", [NT, DS], F32, kind="Internal")

    P = Prog(nc)
    sb = P.sb
    epsc = P.alloc("epsc", [128, 1], F32)
    P.op("vector", lambda q: q.memset(epsc[:, :], LNEPS), inc="dve")

    def tap(nm, src_ap, shape, waits):
        if nm not in debug:
            return
        dbg[nm] = nc.dram_tensor("dbg_" + nm, shape, F32, kind="ExternalOutput").ap()
        P.dma("gpsimd", dbg[nm], src_ap, "out", waits=waits)

    for nm, shape, dt in CONSTS1:
        P.alloc("c_" + nm, shape, dt)
    C = lambda nm: (sb["c_" + nm][...].bitcast(BF16) if nm in BF16_CONSTS
                    else sb["c_" + nm][...])
    gB = P.alloc("gB", [128, D], F32)
    bB = P.alloc("bB", [128, D], F32)
    ssB = P.alloc("ssB", [128, D], F32)
    attn_es = ExitStack()
    KBF = P.alloc("KBF", [128, 4 * 2048], BF16, stack=attn_es)
    VBF = P.alloc("VBF", [128, 16 * 512], BF16, stack=attn_es)
    QBF = P.alloc("QBF", [128, 16 * 1024], BF16, stack=attn_es)
    rows_es = ExitStack()
    for nm, shape, dt in [
            ("negmu_bf", [1, D], BF16), ("rB", [128, D], F32),
            ("rcol", [128, 16], F32), ("wqsum", [1, 2048], BF16),
            ("wksum", [1, 512], BF16), ("wvsum", [1, 512], BF16)]:
        P.alloc(nm, shape, dt, stack=rows_es)

    for nm, shape, dt in CONSTS1:
        P.dma("sync", sb["c_" + nm][...], din[nm], "ld")
    ld_consts = P.cnt["ld"]

    xfm_es = ExitStack()
    XFM = P.alloc("XFM", [128, 16 * 2048], BF16, stack=xfm_es, side="right")
    xfm = lambda dc: XFM[:, dc * 2048:(dc + 1) * 2048]
    xld = []
    for dc in range(16):
        sem = "gx" if dc % 2 == 0 else "gk"
        w = [(sem, P.cnt[sem])] if dc >= 2 else []
        c = P.dma("gpsimd", xfm(dc), din["xT"][dc * 128:(dc + 1) * 128, :],
                  sem, waits=w)
        xld.append((sem, c))

    ph1 = ExitStack()
    for nm in ("murow", "rrow", "row2"):
        P.alloc(nm, [1, D], F32, stack=ph1, side="right")
    xsq = [P.alloc("XSQ0", [128, 2048], BF16, stack=ph1, side="right")] * 2
    ph1a = ExitStack()
    ps_mu = [P.psum(f"ps_mu{i}", [1, 512], stack=ph1a) for i in range(4)]
    ps_sq = [P.psum(f"ps_sq{i}", [1, 512], stack=ph1a) for i in range(4)]

    sq_done, sq_pe = [], []
    for dc in range(16):
        w = [xld[dc]]
        if dc >= 1:
            w.append(("pe", sq_pe[dc - 1]))
        sq_done.append(P.op("scalar", lambda q, dc=dc: q.activation(
            out=xsq[dc % 2][...], in_=xfm(dc), func=AF.Square), waits=w, inc="act",
            drain=False))
        for cb in range(4):
            P.op("tensor", lambda q, dc=dc, cb=cb: q.matmul(
                ps_mu[cb][...], C("ones_col_bf"),
                xfm(dc)[:, cb * 512:(cb + 1) * 512],
                start=(dc == 0), stop=(dc == 15)),
                waits=[xld[dc], ("ld", ld_consts)] if cb == 0 else ())
        for cb in range(4):
            v = P.op("tensor", lambda q, dc=dc, cb=cb: q.matmul(
                ps_sq[cb][...], C("ones_col_bf"),
                xsq[dc % 2][:, cb * 512:(cb + 1) * 512],
                start=(dc == 0), stop=(dc == 15)),
                waits=[("act", sq_done[dc])] if cb == 0 else (),
                inc="pe" if cb == 3 else None)
        sq_pe.append(v)
    pe_stats = P.cnt["pe"]

    for cb in range(4):
        P.op("scalar", lambda q, cb=cb: q.activation(
            out=sb["murow"][0:1, cb * 512:(cb + 1) * 512], in_=ps_mu[cb][...],
            func=AF.Identity, scale=1.0 / D),
            waits=[("pe", pe_stats)] if cb == 0 else (), drain=False)
    for cb in range(4):
        P.op("scalar", lambda q, cb=cb: q.activation(
            out=sb["rrow"][0:1, cb * 512:(cb + 1) * 512], in_=ps_sq[cb][...],
            func=AF.Identity, scale=1.0 / D),
            inc="act" if cb == 3 else None)
    act_ex2 = P.cnt["act"]

    P.op("vector", lambda q: q.tensor_tensor(
        sb["row2"][...], sb["murow"][...], sb["murow"][...], OP.mult),
        waits=[("act", act_ex2)])
    P.op("vector", lambda q: q.tensor_tensor(
        sb["row2"][...], sb["rrow"][...], sb["row2"][...], OP.subtract), inc="dve")
    dve_var = P.cnt["dve"]
    P.op("scalar", lambda q: q.activation(out=sb["row2"][...], in_=sb["row2"][...],
                                          func=AF.Sqrt, bias=epsc[0:1, 0:1]),
         waits=[("dve", dve_var)], inc="act")
    act_sqrt = P.cnt["act"]
    P.op("vector", lambda q: q.reciprocal(sb["rrow"][...], sb["row2"][...]),
         waits=[("act", act_sqrt)])
    P.op("vector", lambda q: q.tensor_single_scalar(
        sb["row2"][...], sb["murow"][...], -1.0, OP.mult))
    P.op("vector", lambda q: q.tensor_copy(sb["negmu_bf"][...], sb["row2"][...]),
         inc="dve")
    dve_rows = P.cnt["dve"]

    ph1a.close()
    ps_bc = [P.psum(f"ps_bc{i}", [128, 512], stack=ph1) for i in range(2)]
    ps_t = P.psum("ps_t", [128, 1], stack=ph1)
    bc_hist = []

    def bcast_rows(dst, src_row, nch, extra_w=()):
        for cb in range(nch):
            bi = len(bc_hist) % 2
            w = list(extra_w) if cb == 0 else []
            if len(bc_hist) >= 2:
                w.append(("dve", bc_hist[-2]))
            pe_v = P.op("tensor", lambda q, cb=cb, src_row=src_row, bi=bi: q.matmul(
                ps_bc[bi][...], C("ones_row_f32"),
                src_row[0:1, cb * 512:(cb + 1) * 512], start=True, stop=True),
                waits=w, inc="pe")
            bc_hist.append(P.op(
                "vector", lambda q, cb=cb, dst=dst, bi=bi:
                q.tensor_copy(dst[:, cb * 512:(cb + 1) * 512], ps_bc[bi][...]),
                waits=[("pe", pe_v)], inc="dve", drain=False))
        return bc_hist[-1]

    bcast_rows(sb["rB"][...], sb["rrow"], 4, extra_w=[("dve", dve_rows)])

    tr_prev = dve_rows
    for tb in range(16):
        pe_v = P.op("tensor", lambda q, tb=tb: q.transpose(
            ps_t[...], sb["rrow"][0:1, tb * 128:(tb + 1) * 128], C("one11")),
            waits=[("dve", tr_prev)], inc="pe")
        tr_prev = P.op("vector", lambda q, tb=tb: q.tensor_copy(
            sb["rcol"][:, tb:tb + 1], ps_t[...]), waits=[("pe", pe_v)], inc="dve")
    ph1_psum_readers = P.cnt["dve"]
    ph1.close()

    ph2 = ExitStack()
    KW = P.alloc("KW", [128, 4 * 2048], BF16, stack=ph2, side="right")
    VW = P.alloc("VW", [128, 16 * 512], BF16, stack=ph2, side="right")
    for ic in range(4):
        P.dma("gpsimd", KW[:, ic * 2048:(ic + 1) * 2048], din["Wk_t"][ic], "gk",
              waits=P.wall() if ic == 0 else ())
    for dc in range(16):
        P.dma("gpsimd", VW[:, dc * 512:(dc + 1) * 512],
              din["Wv"][dc * 128:(dc + 1) * 128, :], "gk")
    gk_all = P.cnt["gk"]

    ps_w = P.psum("ps_w", [1, 512], stack=ph2)
    ps_p = [P.psum(f"ps_p{i}", [128, 512], stack=ph2) for i in range(2)]

    gate2 = P.wall()
    for ic in range(4):
        for dc in range(16):
            w0 = []
            if dc == 0:
                w0 = (gate2 + [("gk", gk_all)]) if ic == 0 \
                    else [("act", P.cnt["act"])]
            v = P.op("tensor", lambda q, ic=ic, dc=dc: q.matmul(
                ps_w[0:1, 0:128], C("ones_col_bf"),
                KW[:, ic * 2048 + dc * 128: ic * 2048 + (dc + 1) * 128],
                start=(dc == 0), stop=(dc == 15)),
                waits=w0,
                inc="pe" if dc == 15 else None)
        P.op("scalar", lambda q, ic=ic: q.activation(
            out=sb["wksum"][0:1, ic * 128:(ic + 1) * 128], in_=ps_w[0:1, 0:128],
            func=AF.Identity), waits=[("pe", v)], inc="act", drain=False)
    for dc in range(16):
        v = P.op("tensor", lambda q, dc=dc: q.matmul(
            ps_w[...], C("ones_col_bf"), VW[:, dc * 512:(dc + 1) * 512],
            start=(dc == 0), stop=(dc == 15)),
            waits=[("gk", gk_all), ("act", P.cnt["act"])] if dc == 0 else (),
            inc="pe" if dc == 15 else None)
    P.op("scalar", lambda q: q.activation(out=sb["wvsum"][...], in_=ps_w[...],
                                          func=AF.Identity),
         waits=[("pe", v)], inc="act", drain=False)
    act_ws = P.cnt["act"]

    kv_evac = []
    pi = 0
    for ic in range(4):
        for tb4 in range(4):
            w = [("act", act_ws), ("dve", ph1_psum_readers)]
            if len(kv_evac) >= 2:
                w.append(("dve", kv_evac[-2]))
            for dc in range(16):
                P.op("tensor", lambda q, ic=ic, tb4=tb4, dc=dc, pi=pi: q.matmul(
                    ps_p[pi][...],
                    KW[:, ic * 2048 + dc * 128: ic * 2048 + (dc + 1) * 128],
                    xfm(dc)[:, tb4 * 512:(tb4 + 1) * 512],
                    start=(dc == 0), stop=False),
                    waits=w if dc == 0 else ())
                w = ()
            pe_v = P.op("tensor", lambda q, ic=ic, tb4=tb4, pi=pi: q.matmul(
                ps_p[pi][...], sb["wksum"][0:1, ic * 128:(ic + 1) * 128],
                sb["negmu_bf"][0:1, tb4 * 512:(tb4 + 1) * 512],
                start=False, stop=True), inc="pe")
            kv_evac.append(P.op("vector", lambda q, ic=ic, tb4=tb4, pi=pi:
                                q.tensor_tensor(
                                    KBF[:, ic * 2048 + tb4 * 512: ic * 2048 + (tb4 + 1) * 512],
                                    ps_p[pi][...],
                                    sb["rB"][:, tb4 * 512:(tb4 + 1) * 512], OP.mult),
                                waits=[("pe", pe_v)], inc="dve", drain=False))
            pi ^= 1

    for tb in range(16):
        w = []
        if len(kv_evac) >= 2:
            w.append(("dve", kv_evac[-2]))
        for dc in range(16):
            P.op("tensor", lambda q, tb=tb, dc=dc, pi=pi: q.matmul(
                ps_p[pi][...], xfm(dc)[:, tb * 128:(tb + 1) * 128],
                VW[:, dc * 512:(dc + 1) * 512], start=(dc == 0), stop=False),
                waits=w if dc == 0 else ())
            w = ()
        pe_v = P.op("tensor", lambda q, tb=tb, pi=pi: q.matmul(
            ps_p[pi][...], sb["negmu_bf"][0:1, tb * 128:(tb + 1) * 128],
            sb["wvsum"][0:1, :], start=False, stop=True), inc="pe")
        kv_evac.append(P.op("vector", lambda q, tb=tb, pi=pi: q.tensor_scalar(
            VBF[:, tb * 512:(tb + 1) * 512], ps_p[pi][...],
            sb["rcol"][:, tb:tb + 1], None, OP.mult, OP.bypass),
            waits=[("pe", pe_v)], inc="dve", drain=False))
        pi ^= 1
    ph2_dve = P.cnt["dve"]
    ph2.close()

    ph3 = ExitStack()
    qwb = [P.alloc(f"qwb{i}", [128, 2048], BF16, stack=ph3, side="right")
           for i in range(2)]
    ps_q = [P.psum(f"ps_q{i}", [128, 512], stack=ph3) for i in range(2)]
    ps_w3 = P.psum("ps_w3", [1, 512], stack=ph3)

    def even_view(ap2d, base):
        v = ap2d[:, base:base + 1024]
        v = v.rearrange("p (k o) -> p k o", k=4)
        return v[:, :, 0:128]

    qw_ld, q_evac, qsum_pe, q_pe_round = [], [], [], []
    gate3 = P.wall()
    for ic in range(16):
        bufi = ic % 2
        wld = [("pe", q_pe_round[ic - 2])] if ic >= 2 else (
            P.wall() if ic == 0 else [])
        qw_ld.append(P.dma("gpsimd", qwb[bufi][...], din["Wq_t"][ic],
                           "q%d" % bufi, waits=wld))
        for dc in range(16):
            w0 = []
            if dc == 0:
                w0 = [("q%d" % bufi, 16 * (ic // 2 + 1))]
                w0 += gate3 if ic == 0 else [("act", P.cnt["act"])]
            v = P.op("tensor", lambda q, bufi=bufi, dc=dc: q.matmul(
                ps_w3[0:1, 0:128], C("ones_col_bf"),
                qwb[bufi][:, dc * 128:(dc + 1) * 128],
                start=(dc == 0), stop=(dc == 15)),
                waits=w0,
                inc="pe" if dc == 15 else None)
        qsum_pe.append(v)
        P.op("scalar", lambda q, ic=ic: q.activation(
            out=sb["wqsum"][0:1, ic * 128:(ic + 1) * 128], in_=ps_w3[0:1, 0:128],
            func=AF.Identity), waits=[("pe", qsum_pe[ic])], inc="act", drain=False)
        act_qs = P.cnt["act"]
        for qh in range(2):
            pqi = qh
            w = [("act", act_qs)]
            if len(q_evac) >= 2:
                w.append(("dve", q_evac[-2]))
            for dc in range(16):
                P.op("tensor", lambda q, bufi=bufi, dc=dc, qh=qh, pqi=pqi: q.matmul(
                    ps_q[pqi][...], qwb[bufi][:, dc * 128:(dc + 1) * 128],
                    even_view(xfm(dc), qh * 1024),
                    start=(dc == 0), stop=False),
                    waits=w if dc == 0 else ())
                w = ()
            pe_v = P.op("tensor", lambda q, ic=ic, qh=qh, pqi=pqi: q.matmul(
                ps_q[pqi][...], sb["wqsum"][0:1, ic * 128:(ic + 1) * 128],
                even_view(sb["negmu_bf"][...], qh * 1024),
                start=False, stop=True), inc="pe")
            q_evac.append(P.op("vector", lambda q, ic=ic, qh=qh, pqi=pqi:
                               q.tensor_tensor(
                                   QBF[:, ic * 1024 + qh * 512: ic * 1024 + (qh + 1) * 512],
                                   ps_q[pqi][...],
                                   even_view(sb["rB"][...], qh * 1024), OP.mult),
                               waits=[("pe", pe_v)], inc="dve", drain=False))
        q_pe_round.append(pe_v)
    ph3.close()

    tap("QBF", QBF[...], [128, 16 * 1024], P.wall())
    tap("KBF", KBF[...], [128, 4 * 2048], P.wall())
    tap("VBF", VBF[...], [128, 16 * 512], P.wall())

    xfm_es.close()
    rows_es.close()

    gb_es = ExitStack()
    ssmat = P.alloc("ssmat", [64, DS], F32, stack=gb_es, side="right")
    gate_f = P.wall()
    P.dma("sync", gB[...],
          din["gam"].rearrange("o d -> (o d)").partition_broadcast(128), "ls")
    P.dma("sync", bB[...],
          din["bet"].rearrange("o d -> (o d)").partition_broadcast(128), "ls")
    P.op("vector", lambda q: q.tensor_reduce(
        ssmat[...], C("slopes").rearrange("p (j g) -> p j g", j=DS), AX.X, OP.add),
        waits=gate_f)
    sv = P.op("vector", lambda q: q.nop(), inc="dve")
    sa = P.op("scalar", lambda q: q.activation(out=ssmat[...], in_=ssmat[...],
                                               func=AF.Sign),
              waits=[("dve", sv)], inc="act")
    d1 = P.dma("sync", ssrow_dram.ap(), ssmat[...], "ls", waits=[("act", sa)])
    bc_last_ls = P.dma(
        "sync", ssB[...],
        ssrow_dram.ap().rearrange("a b -> (a b)").partition_broadcast(128),
        "ls", waits=[("ls", d1)])
    gb_es.close()

    if upto <= 3:
        P.op("sync", lambda q: q.nop(), waits=[("out", P.cnt["out"])] + P.wall())
        P.emit()
        return nc, din, dout, dbg

    ph45 = ExitStack()
    ST8 = P.alloc("ST8", [128, 8 * 24], F32, stack=ph45, side="right")
    S64 = P.alloc("S64", [128, 64], F32, stack=ph45, side="right")
    M8 = {nm: P.alloc(nm, [128, 8], F32, stack=ph45, side="right")
          for nm in ("MU8", "R8", "V8", "SA8", "SB8", "BA8", "RHO8",
                     "TKN8", "TKX8")}
    sc1_es = ExitStack()
    SC1x = [P.alloc(f"SC1x{i}", [128, 2048], F32, stack=sc1_es, side="right")
            for i in range(2)]
    ctx_es = ExitStack()
    CTX = P.alloc("CTX", [128, 16 * 1024], BF16, stack=ctx_es, side="right")
    boB = P.alloc("boB", [128, D], F32, stack=ctx_es, side="right")
    WOB = P.alloc("WOB", [128, 16 * 1024], BF16, stack=ctx_es, side="right")
    ph4 = ExitStack()
    A3 = [P.alloc(f"A3{i}", [128, 512], BF16, stack=ph4, side="right")
          for i in range(3)]
    rs_row = [P.alloc(f"rs_row{i}", [1, 512], F32, stack=ph4, side="right")
              for i in range(2)]
    rb_ctx = [P.alloc(f"rb_ctx{i}", [128, 512], F32, stack=ph4, side="right")
              for i in range(2)]
    BM = P.alloc("BM", [128, 8 * 512], U16, stack=ph4, side="right")
    ps_s = [P.psum(f"ps_s{i}", [128, 512], stack=ph4) for i in range(3)]
    ps_ctx = [P.psum(f"ps_ctx{i}", [128, 512], stack=ph4) for i in range(2)]
    ps_sum = [P.psum(f"ps_sum{i}", [1, 512], stack=ph4) for i in range(2)]
    ps_rb = P.psum("ps_rb", [128, 512], stack=ph4)

    gate4 = P.wall()
    gate3e = [("pe", P.cnt["pe"]), ("act", P.cnt["act"]),
              ("dve", P.cnt["dve"]), ("gp", P.cnt["gp"])]
    bm_ld1 = P.dma("sync", BM[:, 0:1024], din["bmask"][:, 0:1024], "ld",
                   waits=list(gate3e))
    bm_ld2 = P.dma("sync", BM[:, 1024:4096], din["bmask"][:, 1024:4096], "ld")
    bo_ls = P.dma("sync", boB[...],
                  din["bo"].rearrange("o d -> (o d)").partition_broadcast(128),
                  "ls", waits=list(gate3e))
    for hc in range(16):
        P.dma("gpsimd", WOB[:, hc * 1024:(hc + 1) * 1024],
              din["Wo"][hc * 128:(hc + 1) * 128, 0:1024],
              "wo", waits=list(gate3e) if hc == 0 else [])

    exp_done = {}
    mask_gp = {}
    ctx_pe = {}
    recip_d = {-2: 0, -1: 0}
    rbcopy_a = {-1: 0}
    evac_d = {-2: 0, -1: 0}
    groups = [(4 * g + p, qg)
              for g in range(G) for p in range(PHD) for qg in range(2)]
    tiles = []
    for n, (h, qg) in enumerate(groups):
        n_nb = 8 * qg
        tau = n_nb + 8
        for t in range(tau):
            tiles.append((n, h, qg, t, tau, t if t < n_nb else None,
                          None if t < n_nb else t - n_nb))
    NTI = len(tiles)

    def plan_group_tail(n):
        rb_v = P.op("tensor", lambda q, n=n: q.matmul(
            ps_rb[...], C("ones_row_f32"), rs_row[n % 2][...],
            start=True, stop=True),
            waits=[("dve", recip_d[n]), ("act", rbcopy_a[n - 1])], inc="pe")
        rbcopy_a[n] = P.op("scalar", lambda q, n=n: q.activation(
            out=rb_ctx[n % 2][...], in_=ps_rb[...], func=AF.Identity),
            waits=[("pe", rb_v), ("dve", evac_d[n - 2])], inc="act", drain=False)
        h, qg = groups[n]
        evac_d[n] = P.op("vector", lambda q, h=h, qg=qg, n=n: q.tensor_tensor(
            CTX[:, h * 1024 + qg * 512: h * 1024 + (qg + 1) * 512],
            ps_ctx[n % 2][...], rb_ctx[n % 2][...], OP.mult),
            waits=[("act", rbcopy_a[n])], inc="dve", drain=False)

    def plan_score_exp(i):
        n, h, qg, t, tau, kb_nb, kbp = tiles[i]
        g = h // 4
        qbase = h * 1024 + qg * 512
        kb = kb_nb if kbp is None else 8 * qg + kbp
        si = i % 3
        w = list(gate4) if i == 0 else []
        if i >= 3:
            w.append(("act", exp_done[i - 3]))
        pe_v = P.op("tensor", lambda q, g=g, kb=kb, qbase=qbase, si=si:
                    q.matmul(ps_s[si][...],
                             KBF[:, g * 2048 + kb * 128: g * 2048 + (kb + 1) * 128],
                             QBF[:, qbase:qbase + 512],
                             start=True, stop=True),
                    waits=w, inc="pe")
        exp_done[i] = P.op("scalar", lambda q, si=si, ab=si:
                           q.activation(out=A3[ab][...], in_=ps_s[si][...],
                                        func=AF.Exp, scale=ISQD),
                           waits=[("pe", pe_v)], inc="act", drain=False)
        if kbp is not None:
            mask_gp[i] = P.op("gpsimd", lambda q, ab=si, kbp=kbp: q.tensor_mul(
                A3[ab][...], A3[ab][...],
                BM[:, kbp * 512:(kbp + 1) * 512].bitcast(BF16)),
                waits=[("act", exp_done[i]),
                       ("ld", bm_ld1 if kbp < 2 else bm_ld2)], inc="gp")

    def plan_sum_ctx(i):
        n, h, qg, t, tau, kb_nb, kbp = tiles[i]
        g = h // 4
        kb = kb_nb if kbp is None else 8 * qg + kbp
        ab = i % 3
        ws = [("gp", mask_gp[i])] if kbp is not None else [("act", exp_done[i])]
        if t == 0:
            ws.append(("dve", recip_d[n - 2]))
        P.op("tensor", lambda q, ab=ab, t=t, tau=tau, n=n: q.matmul(
            ps_sum[n % 2][...], C("ones_col_bf"), A3[ab][...],
            start=(t == 0), stop=(t == tau - 1)), waits=ws)
        wc = [("dve", evac_d[n - 2])] if t == 0 else ()
        ctx_pe[i] = P.op("tensor", lambda q, g=g, kb=kb, ab=ab, t=t,
                         tau=tau, n=n: q.matmul(
            ps_ctx[n % 2][...],
            VBF[:, kb * 512 + g * 128: kb * 512 + (g + 1) * 128],
            A3[ab][...], start=(t == 0), stop=(t == tau - 1)),
            waits=wc, inc="pe")
        if t == tau - 1:
            recip_d[n] = P.op("vector", lambda q, n=n: q.reciprocal(
                rs_row[n % 2][...], ps_sum[n % 2][...]),
                waits=[("pe", P.cnt["pe"])], inc="dve")
            if n >= 1:
                plan_group_tail(n - 1)

    for i in range(NTI):
        plan_score_exp(i)
        if i >= 2:
            plan_sum_ctx(i - 2)
    plan_sum_ctx(NTI - 2)
    plan_sum_ctx(NTI - 1)
    plan_group_tail(len(groups) - 1)
    ph4.close()
    attn_es.close()
    tap("CTX", CTX[...], [128, 16 * 1024], P.wall())
    if upto <= 4:
        P.op("sync", lambda q: q.nop(), waits=[("out", P.cnt["out"])] + P.wall())
        P.emit()
        return nc, din, dout, dbg

    X2 = P.alloc("X2", [128, 8 * 2048], F32)
    x2t = lambda tb: X2[:, tb * 2048:(tb + 1) * 2048]
    XSL = P.alloc("XSL", [128, 8 * DS], F32)
    TOA = P.alloc("TOA", [128, 8 * 64], F32)
    OHT = P.alloc("OHT", [64, 8 * 128], F32)
    STK = P.alloc("STK", [128, 24], F32)
    TMIN = P.alloc("TMIN", [128, 64], F32)
    TMAX = P.alloc("TMAX", [128, 64], F32)
    SEND = P.alloc("SEND", [128, 1], F32)

    ph5 = ExitStack()
    XQ = [P.alloc(f"XQ{i}", [128, 2048], F32, stack=ph5, side="right")
          for i in range(2)]
    ps_o = [P.psum(f"ps_o{i}", [128, 512], stack=ph5) for i in range(2)]
    gate5 = [("pe", P.cnt["pe"]), ("act", P.cnt["act"]),
             ("dve", P.cnt["dve"]), ("gp", P.cnt["gp"])]

    MU8, R8, V8 = M8["MU8"][...], M8["R8"][...], M8["V8"][...]
    SA8, SB8, BA8 = M8["SA8"][...], M8["SB8"][...], M8["BA8"][...]

    def bn1_tb(x2t_, tb, waits):
        for cb in range(4):
            P.op("vector", lambda q, tb=tb, cb=cb: q.bn_stats(
                ST8[:, tb * 24 + cb * 6: tb * 24 + (cb + 1) * 6],
                x2t_(tb)[:, cb * 512:(cb + 1) * 512]),
                waits=waits if cb == 0 else (), drain=False)

    def agg_half(h, waits=()):
        sl = slice(h * 4, h * 4 + 4)
        v4 = ST8[:, h * 96:(h + 1) * 96].rearrange(
            "p (t a b) -> p t a b", t=4, a=8)
        mus = v4[:, :, :, 1:2].squeeze()
        m2s = v4[:, :, :, 2:3].squeeze()
        S64h = S64[:, h * 32:(h + 1) * 32]
        P.op("vector", lambda q: q.nop(), waits=waits)
        P.op("vector", lambda q: q.tensor_tensor(S64h, mus, mus, OP.mult))
        P.op("vector", lambda q: q.tensor_reduce(
            SA8[:, sl], S64h.rearrange("p (t a) -> p t a", t=4), AX.X, OP.add))
        P.op("vector", lambda q: q.tensor_reduce(SB8[:, sl], m2s, AX.X, OP.add))
        P.op("vector", lambda q: q.scalar_tensor_tensor(
            SB8[:, sl], SA8[:, sl], 256.0, SB8[:, sl], OP.mult, OP.add))
        P.op("vector", lambda q: q.tensor_reduce(MU8[:, sl], mus, AX.X, OP.add))
        P.op("vector", lambda q: q.tensor_single_scalar(
            MU8[:, sl], MU8[:, sl], 1.0 / 8.0, OP.mult))
        P.op("vector", lambda q: q.tensor_tensor(SA8[:, sl], MU8[:, sl],
                                                 MU8[:, sl], OP.mult))
        P.op("vector", lambda q: q.tensor_single_scalar(
            V8[:, sl], SB8[:, sl], 1.0 / 2048.0, OP.mult))
        v = P.op("vector", lambda q: q.tensor_tensor(
            V8[:, sl], V8[:, sl], SA8[:, sl], OP.subtract), inc="dve")
        a = P.op("scalar", lambda q: q.activation(
            out=R8[:, sl], in_=V8[:, sl], func=AF.Sqrt, bias=sb["epsc"][:, :]),
            waits=[("dve", v)], inc="act")
        return P.op("vector", lambda q: q.reciprocal(R8[:, sl], R8[:, sl]),
                    waits=[("act", a)], inc="dve")

    xq_ld = []
    res_gp = []

    def plan_res(tb, bvh):
        wl = [("gp", res_gp[tb - 2])] if tb >= 2 else gate5
        xq_ld.append(P.dma("sync", XQ[tb % 2][...],
                           din["xq"][tb * 128:(tb + 1) * 128, :], "lx", waits=wl))
        wsc = [("dve", bvh)]
        if tb >= 2:
            wsc.append(("gp", res_gp[tb - 2]))
        sc1_act = P.op("scalar", lambda q, tb=tb: q.activation(
            out=SC1x[tb % 2][...], in_=x2t(tb), func=AF.Identity,
            scale=M8["R8"][:, tb:tb + 1], bias=M8["BA8"][:, tb:tb + 1]),
            waits=wsc, inc="act", drain=False)
        res_gp.append(P.op("gpsimd", lambda q, tb=tb: q.tensor_add(
            x2t(tb), SC1x[tb % 2][...], XQ[tb % 2][...]),
            waits=[("act", sc1_act), ("lx", xq_ld[tb])], inc="gp"))
        P.dma("sync", dout["o_x2"].rearrange("(a p) d -> p a d", p=128)
              [:, tb:tb + 1, :],
              x2t(tb).rearrange("p (o d) -> p o d", o=1),
              "out", waits=[("gp", res_gp[tb])])

    def stats1_half(h):
        rv = agg_half(h)
        P.op("vector", lambda q, h=h: q.tensor_tensor(
            BA8[:, h * 4:(h + 1) * 4], MU8[:, h * 4:(h + 1) * 4],
            R8[:, h * 4:(h + 1) * 4], OP.mult))
        bvh = P.op("vector", lambda q, h=h: q.tensor_single_scalar(
            BA8[:, h * 4:(h + 1) * 4], BA8[:, h * 4:(h + 1) * 4], -1.0,
            OP.mult), inc="dve")
        for tb in range(h * 4, h * 4 + 4):
            plan_res(tb, bvh)

    o_evac = {}
    oi = 0
    qsems = ["wo", "q0", "q1", "gx"]
    wq_cnt = {}
    for jh in range(2):
        if jh == 1:
            for hc in range(16):
                wld = []
                if hc == 0:
                    wld.append(("pe", P.cnt["pe"]))
                sem = qsems[hc // 4]
                P.dma("gpsimd", WOB[:, hc * 1024:(hc + 1) * 1024],
                      din["Wo"][hc * 128:(hc + 1) * 128,
                                jh * 1024:(jh + 1) * 1024],
                      sem, waits=wld)
                wq_cnt[hc // 4] = P.cnt[sem]
        for tb in range(8):
            for jb in range(2):
                si = oi % 2
                w = [("wo", 256)] if jh == 0 else []
                if oi >= 2:
                    w.append(("dve", o_evac[oi - 2]))
                for hc in range(16):
                    if jh == 1 and (tb, jb) == (0, 0) and hc % 4 == 0:
                        w.append((qsems[hc // 4], wq_cnt[hc // 4]))
                    v = P.op("tensor", lambda q, hc=hc, tb=tb, jb=jb, si=si: q.matmul(
                        ps_o[si][...],
                        CTX[:, hc * 1024 + tb * 128: hc * 1024 + (tb + 1) * 128],
                        WOB[:, hc * 1024 + jb * 512: hc * 1024 + (jb + 1) * 512],
                        start=(hc == 0), stop=(hc == 15)),
                        waits=w if (hc == 0 or w) else (),
                        inc="pe" if hc == 15 else None)
                    w = []
                jcol = jh * 1024 + jb * 512
                o_evac[oi] = P.op("vector", lambda q, tb=tb, jcol=jcol, si=si:
                                  q.tensor_tensor(
                                      x2t(tb)[:, jcol:jcol + 512], ps_o[si][...],
                                      boB[:, jcol:jcol + 512], OP.add),
                                  waits=[("pe", v)] +
                                  ([("ls", bo_ls)] if oi == 0 else []),
                                  inc="dve", drain=False)
                oi += 1
            if jh == 1:
                bn1_tb(x2t, tb, [("dve", o_evac[oi - 1])])
                if tb == 3:
                    stats1_half(0)
    stats1_half(1)
    o_all = P.cnt["dve"]
    if upto <= 4.7:
        tap("X2", X2[...], [128, 8 * 2048], P.wall())
        P.op("sync", lambda q: q.nop(), waits=[("out", P.cnt["out"])] + P.wall())
        P.emit()
        return nc, din, dout, dbg
    ph5.close()
    ctx_es.close()
    sc1_es.close()
    tap("X2", X2[...], [128, 8 * 2048], P.wall())
    if upto <= 5:
        P.op("sync", lambda q: q.nop(), waits=[("out", P.cnt["out"])] + P.wall())
        P.emit()
        return nc, din, dout, dbg

    ph6 = ExitStack()
    XN = P.alloc("XN", [128, 8 * 2048], F32, stack=ph6, side="right")
    xnt = lambda tb: XN[:, tb * 2048:(tb + 1) * 2048]
    SC2x = [P.alloc(f"SC2x{i}", [128, 2048], F32, stack=ph6, side="right")
            for i in range(4)]
    TSall = P.alloc("TSall", [128, 512], F32, stack=ph6, side="right")
    K64a = P.alloc("K64a", [128, 512], F32, stack=ph6, side="right")
    CS64 = P.alloc("CS64", [128, 64], F32, stack=ph6, side="right")
    CM8 = P.alloc("CM8", [128, 8], F32, stack=ph6, side="right")
    ps_tr = P.psum("ps_tr", [64, 128], stack=ph6)

    ts_done = {}
    gam_done = {}
    for h in range(2):
        sl = slice(h * 4, (h + 1) * 4)
        for tb in range(h * 4, h * 4 + 4):
            bn1_tb(x2t, tb, [("gp", res_gp[tb])])
        agg_half(h)
        P.op("vector", lambda q, h=h, sl=sl: q.tensor_copy(
            STK[:, h * 4:h * 4 + 4], MU8[:, sl]), drain=False)
        P.op("vector", lambda q, h=h, sl=sl: q.tensor_copy(
            STK[:, 8 + h * 4:8 + h * 4 + 4], R8[:, sl]), drain=False)
        P.op("vector", lambda q, h=h, sl=sl: q.tensor_copy(
            STK[:, 16 + h * 4:16 + h * 4 + 4], V8[:, sl]), drain=False)
        RHO = M8["RHO8"][...]
        P.op("vector", lambda q, sl=sl: q.tensor_tensor(
            SA8[:, sl], R8[:, sl], R8[:, sl], OP.mult))
        P.op("vector", lambda q, sl=sl: q.tensor_tensor(
            SA8[:, sl], SA8[:, sl], V8[:, sl], OP.mult), inc="dve")
        va = P.op("scalar", lambda q, sl=sl: q.activation(
            out=SA8[:, sl], in_=SA8[:, sl], func=AF.Sqrt,
            bias=sb["epsc"][:, :]),
            waits=[("dve", P.cnt["dve"])], inc="act")
        P.op("vector", lambda q, sl=sl: q.reciprocal(SA8[:, sl], SA8[:, sl]),
             waits=[("act", va)])
        P.op("vector", lambda q, sl=sl, RHO=RHO: q.tensor_tensor(
            RHO[:, sl], R8[:, sl], SA8[:, sl], OP.mult))
        P.op("vector", lambda q, sl=sl, RHO=RHO: q.tensor_tensor(
            BA8[:, sl], MU8[:, sl], RHO[:, sl], OP.mult))
        bv6h = P.op("vector", lambda q, sl=sl: q.tensor_single_scalar(
            BA8[:, sl], BA8[:, sl], -1.0, OP.mult), inc="dve")

        tbs = list(range(h * 4, h * 4 + 4))
        zs, gams, bets, sss, sgs = {}, {}, {}, {}, {}
        for tb in tbs:
            zs[tb] = P.op("scalar", lambda q, tb=tb: q.activation(
                out=xnt(tb), in_=x2t(tb), func=AF.Identity,
                scale=M8["RHO8"][:, tb:tb + 1], bias=M8["BA8"][:, tb:tb + 1]),
                waits=[("dve", bv6h)], inc="act", drain=False)
        for tb in tbs:
            gams[tb] = P.op("vector", lambda q, tb=tb: q.tensor_tensor(
                xnt(tb), xnt(tb), gB[...], OP.mult),
                waits=[("act", zs[tb])] +
                ([("ls", bc_last_ls)] if tb == 0 else []),
                inc="dve", drain=False)
        for tb in tbs:
            bets[tb] = P.op("gpsimd", lambda q, tb=tb: q.tensor_add(
                xnt(tb), xnt(tb), bB[...]), waits=[("dve", gams[tb])], inc="gp")
        for tb in tbs:
            sss[tb] = P.op("vector", lambda q, tb=tb: q.tensor_mul(
                SC2x[tb % 4][...], xnt(tb), ssB[...]),
                waits=[("gp", bets[tb])] +
                ([("dve", ts_done[tb - 4])] if tb >= 4 else []),
                inc="dve", drain=False)
        for tb in tbs:
            sgs[tb] = P.op("scalar", lambda q, tb=tb: q.activation(
                out=SC2x[tb % 4][...], in_=SC2x[tb % 4][...], func=AF.Sign),
                waits=[("dve", sss[tb])], inc="act", drain=False)
        for tb in tbs:
            ts_done[tb] = P.op("vector", lambda q, tb=tb: q.tensor_reduce(
                TSall[:, tb * 64:(tb + 1) * 64],
                SC2x[tb % 4][...].rearrange("p (i j) -> p i j", i=64),
                AX.X, OP.add),
                waits=[("act", sgs[tb])], inc="dve")

    P.op("vector", lambda q: q.tensor_reduce(
        CS64[...], TSall[...].rearrange("p (t c i) -> p t c i", t=8, c=8),
        AX.X, OP.add), waits=[("dve", ts_done[7])])
    P.op("vector", lambda q: q.tensor_single_scalar(CS64[...], CS64[...], 8.0,
                                                    OP.mult))
    P.op("vector", lambda q: q.tensor_tensor(
        CS64[...].rearrange("p (t c) -> p t c", t=8),
        CS64[...].rearrange("p (t c) -> p t c", t=8),
        C("iota8").unsqueeze(1).broadcast_to([128, 8, 8]), OP.subtract))
    P.op("vector", lambda q: q.tensor_reduce(
        CM8[...], CS64[...].rearrange("p (t c) -> p t c", t=8), AX.X, OP.max))
    P.op("vector", lambda q: q.tensor_tensor(
        CS64[...].rearrange("p (t c) -> p t c", t=8),
        CS64[...].rearrange("p (t c) -> p t c", t=8),
        CM8[...].unsqueeze(2).broadcast_to([128, 8, 8]), OP.is_equal))
    P.op("vector", lambda q: q.tensor_single_scalar(K64a[...], TSall[...], 64.0,
                                                    OP.mult))
    P.op("vector", lambda q: q.scalar_tensor_tensor(
        K64a[...].rearrange("p (t c i) -> p t c i", t=8, c=8),
        CS64[...].rearrange("p (t c) -> p t c", t=8).unsqueeze(3)
        .broadcast_to([128, 8, 8, 8]),
        16384.0,
        K64a[...].rearrange("p (t c i) -> p t c i", t=8, c=8), OP.mult, OP.add))
    P.op("vector", lambda q: q.tensor_tensor(
        K64a[...].rearrange("p (t i) -> p t i", t=8),
        K64a[...].rearrange("p (t i) -> p t i", t=8),
        C("iota64").unsqueeze(1).broadcast_to([128, 8, 64]), OP.subtract))
    P.op("vector", lambda q: q.tensor_reduce(
        SA8, K64a[...].rearrange("p (t i) -> p t i", t=8), AX.X, OP.max))
    tov = P.op("vector", lambda q: q.tensor_tensor(
        TOA[...].rearrange("p (t i) -> p t i", t=8),
        K64a[...].rearrange("p (t i) -> p t i", t=8),
        SA8.unsqueeze(2).broadcast_to([128, 8, 64]), OP.is_equal), inc="dve")

    prev_tr = 0
    for tb in range(8):
        trv = P.op("tensor", lambda q, tb=tb: q.transpose(
            ps_tr[...], TOA[:, tb * 64:(tb + 1) * 64], C("ident_f32")),
            waits=[("dve", tov), ("dve", prev_tr)], inc="pe")
        prev_tr = P.op("vector", lambda q, tb=tb: q.tensor_copy(
            OHT[:, tb * 128:(tb + 1) * 128], ps_tr[...]),
            waits=[("pe", trv)], inc="dve", drain=False)

    xsl_done = {}
    for hh in range(2):
        for tb in range(hh * 4, hh * 4 + 4):
            w = [("dve", tov)]
            if tb >= 4:
                w.append(("dve", xsl_done[tb - 4]))
            P.op("vector", lambda q, tb=tb: q.tensor_tensor(
                SC2x[tb % 4][...].rearrange("p (i j) -> p i j", i=64),
                xnt(tb).rearrange("p (i j) -> p i j", i=64),
                TOA[:, tb * 64:(tb + 1) * 64].unsqueeze(2)
                .broadcast_to([128, 64, DS]),
                OP.mult), waits=w)
        for tb in range(hh * 4, hh * 4 + 4):
            xsl_done[tb] = P.op("vector", lambda q, tb=tb: q.tensor_reduce(
                XSL[:, tb * DS:(tb + 1) * DS],
                SC2x[tb % 4][...].rearrange("p (i j) -> p j i", i=64),
                AX.X, OP.add),
                inc="dve")

    P.op("vector", lambda q: q.tensor_reduce(
        M8["TKN8"][...], XSL[...].rearrange("p (t j) -> p t j", t=8), AX.X, OP.min),
        waits=[("dve", xsl_done[7])])
    P.op("vector", lambda q: q.tensor_reduce(
        M8["TKX8"][...], XSL[...].rearrange("p (t j) -> p t j", t=8), AX.X, OP.max))
    P.op("vector", lambda q: q.tensor_tensor(
        K64a[...].rearrange("p (t i) -> p t i", t=8),
        TOA[...].rearrange("p (t i) -> p t i", t=8),
        M8["TKN8"][...].unsqueeze(2).broadcast_to([128, 8, 64]), OP.mult))
    P.op("vector", lambda q: q.tensor_scalar(
        TSall[...], TOA[...], -3e38, 3e38, OP.mult, OP.add))
    P.op("vector", lambda q: q.tensor_tensor(K64a[...], K64a[...], TSall[...],
                                             OP.add))
    P.op("vector", lambda q: q.tensor_reduce(
        TMIN[...], K64a[...].rearrange("p (t i) -> p i t", t=8), AX.X, OP.min))
    P.op("vector", lambda q: q.tensor_tensor(
        K64a[...].rearrange("p (t i) -> p t i", t=8),
        TOA[...].rearrange("p (t i) -> p t i", t=8),
        M8["TKX8"][...].unsqueeze(2).broadcast_to([128, 8, 64]), OP.mult))
    P.op("vector", lambda q: q.tensor_scalar(
        TSall[...], TOA[...], 3e38, -3e38, OP.mult, OP.add))
    P.op("vector", lambda q: q.tensor_tensor(K64a[...], K64a[...], TSall[...],
                                             OP.add))
    xs_done = P.op("vector", lambda q: q.tensor_reduce(
        TMAX[...], K64a[...].rearrange("p (t i) -> p i t", t=8), AX.X, OP.max),
        inc="dve")

    trv = P.op("tensor", lambda q: q.transpose(ps_tr[...], TMAX[...], C("ident_f32")),
               waits=[("dve", xs_done), ("dve", prev_tr)], inc="pe")
    mv = P.op("vector", lambda q: q.tensor_reduce(SEND[0:64, 0:1], ps_tr[...],
                                                  AX.X, OP.max),
              waits=[("pe", trv)], inc="dve")
    trv2 = P.op("tensor", lambda q: q.transpose(ps_tr[...], TMIN[...], C("ident_f32")),
                waits=[("dve", mv)], inc="pe")
    mv2 = P.op("vector", lambda q: q.tensor_reduce(SEND[64:128, 0:1], ps_tr[...],
                                                   AX.X, OP.min, negate=True),
               waits=[("pe", trv2)], inc="dve")

    P.dma("sync", dout["o_partials"], SEND[...], "out", waits=[("dve", mv2)])
    P.dma("sync", dout["o_xsl"], XSL[...], "out",
          waits=[("dve", xsl_done[7])])
    P.dma("sync", dout["o_to"], TOA[...], "out", waits=[("dve", tov)])
    P.dma("sync", dout["o_onehotT"], OHT[...], "out",
          waits=[("dve", prev_tr)])
    P.dma("sync", dout["o_stats"], STK[...], "out",
          waits=[("dve", ts_done[7])])
    P.op("sync", lambda q: q.nop(), waits=[("out", P.cnt["out"])])

    P.emit()
    return nc, din, dout, dbg


def build_launch2(debug=()):
    nc = bass.Bass("TRN2", target_bir_lowering=False, detect_race_conditions=False)
    din = {nm: nc.dram_tensor(nm, shape, dt, kind="ExternalInput").ap()
           for nm, shape, dt in IN2_SPECS}
    dout = nc.dram_tensor("o_out", [TQ, D], F32, kind="ExternalOutput").ap()
    dbg = {}

    P = Prog(nc)
    sb = P.sb
    epsc = P.alloc("epsc", [128, 1], F32)
    P.op("vector", lambda q: q.memset(epsc[:, :], LNEPS), inc="dve")
    for nm, shape, dt in CONSTS2:
        P.alloc("c_" + nm, shape, dt)
    C = lambda nm: sb["c_" + nm][...]

    X2 = P.alloc("X2", [128, 8 * 2048], F32)
    x2t = lambda tb: X2[:, tb * 2048:(tb + 1) * 2048]
    XSL = P.alloc("XSL", [128, 8 * DS], F32)
    TOA = P.alloc("TOA", [128, 8 * 64], F32)
    OHT = P.alloc("OHT", [64, 8 * 128], F32)
    STK = P.alloc("STK", [128, 24], F32)
    PART = P.alloc("PART", [128, NC], F32)
    GM = P.alloc("GM", [128, 1], F32)
    TMINC = P.alloc("TMINC", [64, 1], F32)
    M8 = {nm: P.alloc(nm, [128, 8], F32)
          for nm in ("MN8", "MX8", "SC8", "INV8", "R2P8", "VY8", "MUF8", "RF8",
                     "CR8", "DD8", "SCL8", "BO8", "T8")}
    SEL = P.alloc("SEL", [128, 32], F32)
    U256 = P.alloc("U256", [128, 8 * DS], F32)
    GS256 = P.alloc("GS256", [128, 8 * DS], F32)
    XL256 = P.alloc("XL256", [128, 8 * DS], F32)
    B256 = P.alloc("B256", [128, 8 * DS], F32)
    S256 = P.alloc("S256", [128, 8 * DS], F32)
    DEL = P.alloc("DEL", [128, 8 * DS], F32)
    Y2S = P.alloc("Y2S", [128, 8 * DS], F32)
    GSEL = P.alloc("GSEL", [128, 8 * DS], F32)
    CMPX = P.alloc("CMPX", [128, 8 * DS * GRID], F32)
    INDX = P.alloc("INDX", [128, 8 * DS * GRID], F32)
    TBL = P.alloc("TBL", [128, 8 * DS * GRID], F32)
    TBL2 = P.alloc("TBL2", [128, 8 * DS * GRID], F32)
    SC2x = [P.alloc(f"SC2x{i}", [128, 2048], F32) for i in range(2)]
    OUTT = [P.alloc(f"OUTT{i}", [128, 2048], F32) for i in range(2)]

    ps_m = P.psum("ps_m", [128, 32])
    ps_g = P.psum("ps_g", [128, 512])
    ps_tb = [P.psum(f"ps_tb{i}", [128, 512]) for i in range(2)]

    for nm, shape, dt in CONSTS2:
        P.dma("sync", C(nm), din[nm], "ld")
    P.dma("sync", XSL[...], din["xsl"], "ld")
    P.dma("sync", TOA[...], din["to_all"], "ld")
    P.dma("sync", OHT[...], din["onehotT"], "ld")
    P.dma("sync", STK[...], din["stats"], "ld")
    P.dma("sync", PART[...], din["partials"], "ld")
    ld_all = P.cnt["ld"]
    x2_ld = []
    for tb in range(8):
        x2_ld.append(P.dma(
            "sync", x2t(tb).rearrange("p (o d) -> p o d", o=1),
            din["x2"].rearrange("(a p) d -> p a d", p=128)[:, tb:tb + 1, :],
            "ld"))
    mu2 = STK[:, 0:8]
    r2 = STK[:, 8:16]
    v2 = STK[:, 16:24]

    P.op("vector", lambda q: q.tensor_reduce(GM[...], PART[...], AX.X, OP.max),
         waits=[("ld", ld_all)])
    gv = P.op("vector", lambda q: q.tensor_single_scalar(
        TMINC[...], GM[64:128, 0:1], -1.0, OP.mult), inc="dve")

    sel_pe = []
    for tb in range(8):
        oht = OHT[:, tb * 128:(tb + 1) * 128]
        for i, rhs in enumerate([TMINC[...], GM[0:64, 0:1], C("oscale")]):
            v = P.op("tensor", lambda q, oht=oht, rhs=rhs, tb=tb, i=i: q.matmul(
                ps_m[:, tb * 3 + i: tb * 3 + i + 1], oht, rhs,
                start=True, stop=True),
                waits=[("dve", gv)] if (tb == 0 and i == 0) else (), inc="pe")
    sel_pe = P.cnt["pe"]
    e1 = P.op("vector", lambda q: q.tensor_copy(SEL[:, 0:24], ps_m[:, 0:24]),
              waits=[("pe", sel_pe)], inc="dve")
    mn3 = SEL[:, 0:24].rearrange("p (t k) -> p t k", t=8)
    P.op("vector", lambda q: q.tensor_tensor(
        M8["MN8"][...], mn3[:, :, 0:1].squeeze(), mn3[:, :, 0:1].squeeze(), OP.max),
        waits=[("dve", e1)])
    P.op("vector", lambda q: q.tensor_tensor(
        M8["MX8"][...], mn3[:, :, 1:2].squeeze(), mn3[:, :, 1:2].squeeze(), OP.max))
    P.op("vector", lambda q: q.tensor_tensor(
        M8["SC8"][...], mn3[:, :, 2:3].squeeze(), mn3[:, :, 2:3].squeeze(), OP.max))
    for tb in range(8):
        oht = OHT[:, tb * 128:(tb + 1) * 128]
        P.op("tensor", lambda q, oht=oht, tb=tb: q.matmul(
            ps_g[:, tb * DS:(tb + 1) * DS], oht, C("gam_rs"),
            start=True, stop=True), inc="pe")
    gp_pe = P.cnt["pe"]
    e2 = P.op("vector", lambda q: q.tensor_copy(GSEL[...], ps_g[:, 0:256]),
              waits=[("pe", gp_pe)], inc="dve")
    for tb in range(8):
        oht = OHT[:, tb * 128:(tb + 1) * 128]
        P.op("tensor", lambda q, oht=oht, tb=tb: q.matmul(
            ps_g[:, tb * DS:(tb + 1) * DS], oht, C("bet_rs"),
            start=True, stop=True), waits=[("dve", e2)] if tb == 0 else (),
            inc="pe")
    bp_pe = P.cnt["pe"]
    P.op("vector", lambda q: q.tensor_tensor(Y2S[...], XSL[...], ps_g[:, 0:256],
                                             OP.subtract),
         waits=[("pe", bp_pe)], inc="dve")
    tbl_evac = {}
    for tb in range(8):
        oht = OHT[:, tb * 128:(tb + 1) * 128]
        v = P.op("tensor", lambda q, oht=oht, tb=tb: q.matmul(
            ps_tb[tb % 2][...], oht, C("bases"), start=True, stop=True),
            waits=[("dve", tbl_evac[tb - 2])] if tb >= 2 else (), inc="pe")
        tbl_evac[tb] = P.op("vector", lambda q, tb=tb: q.tensor_copy(
            TBL[:, tb * 512:(tb + 1) * 512], ps_tb[tb % 2][...]),
            waits=[("pe", v)], inc="dve", drain=False)
    bases_done = P.cnt["dve"]

    P.op("vector", lambda q: q.tensor_tensor(M8["INV8"][...], M8["MX8"][...],
                                             M8["MN8"][...], OP.subtract))
    P.op("vector", lambda q: q.tensor_single_scalar(
        M8["INV8"][...], M8["INV8"][...], 1e-8, OP.add))
    P.op("vector", lambda q: q.reciprocal(M8["INV8"][...], M8["INV8"][...]))
    P.op("vector", lambda q: q.tensor_tensor(
        U256[...].rearrange("p (t j) -> p t j", t=8),
        XSL[...].rearrange("p (t j) -> p t j", t=8),
        M8["MN8"][...].unsqueeze(2).broadcast_to([128, 8, DS]), OP.subtract))
    P.op("vector", lambda q: q.tensor_tensor(
        U256[...].rearrange("p (t j) -> p t j", t=8),
        U256[...].rearrange("p (t j) -> p t j", t=8),
        M8["INV8"][...].unsqueeze(2).broadcast_to([128, 8, DS]), OP.mult))
    P.op("vector", lambda q: q.tensor_scalar(
        U256[...], U256[...], 0.0, CLIPHI, OP.max, OP.min))
    P.op("vector", lambda q: q.tensor_tensor(
        CMPX[...].rearrange("p (t j g) -> p t j g", t=8, j=DS),
        U256[...].rearrange("p (t j) -> p t j", t=8).unsqueeze(3)
        .broadcast_to([128, 8, DS, GRID]),
        C("grid_cmp").rearrange("p (j g) -> p j g", j=DS).unsqueeze(1)
        .broadcast_to([128, 8, DS, GRID]), OP.is_ge))
    P.op("vector", lambda q: q.tensor_reduce(
        GS256[...], CMPX[...].rearrange("p (t j g) -> p (t j) g", t=8, j=DS),
        AX.X, OP.add))
    P.op("vector", lambda q: q.tensor_single_scalar(XL256[...], U256[...], 16.0,
                                                    OP.mult))
    P.op("vector", lambda q: q.tensor_tensor(XL256[...], XL256[...], GS256[...],
                                             OP.subtract))
    P.op("vector", lambda q: q.tensor_single_scalar(XL256[...], XL256[...], 1.0,
                                                    OP.add))
    cmp4 = CMPX[...].rearrange("p (x g) -> p x g", g=GRID)
    ind4 = INDX[...].rearrange("p (x g) -> p x g", g=GRID)
    P.op("vector", lambda q: q.tensor_tensor(
        ind4[:, :, 0:GRID - 1], cmp4[:, :, 0:GRID - 1], cmp4[:, :, 1:GRID],
        OP.subtract))
    iv = P.op("vector", lambda q: q.tensor_tensor(
        ind4[:, :, GRID - 1:GRID], cmp4[:, :, GRID - 1:GRID],
        cmp4[:, :, GRID - 1:GRID], OP.max), inc="dve")
    P.op("vector", lambda q: q.tensor_tensor(CMPX[...], INDX[...], TBL[...],
                                             OP.mult),
         waits=[("dve", bases_done)])
    bsel_done = P.op("vector", lambda q: q.tensor_reduce(
        B256[...], CMPX[...].rearrange("p (x g) -> p x g", g=GRID), AX.X, OP.add),
        inc="dve")
    tbl_evac = {}
    for tb in range(8):
        oht = OHT[:, tb * 128:(tb + 1) * 128]
        w = ([("dve", tbl_evac[tb - 2])] if tb >= 2
             else [("dve", bases_done)])
        v = P.op("tensor", lambda q, oht=oht, tb=tb: q.matmul(
            ps_tb[tb % 2][...], oht, C("slopes"), start=True, stop=True),
            waits=w, inc="pe")
        tbl_evac[tb] = P.op("vector", lambda q, tb=tb: q.tensor_copy(
            TBL2[:, tb * 512:(tb + 1) * 512], ps_tb[tb % 2][...]),
            waits=[("pe", v)], inc="dve", drain=False)
    slopes_done = P.cnt["dve"]
    P.op("vector", lambda q: q.tensor_tensor(CMPX[...], INDX[...], TBL2[...],
                                             OP.mult),
         waits=[("dve", slopes_done)])
    P.op("vector", lambda q: q.tensor_reduce(
        S256[...], CMPX[...].rearrange("p (x g) -> p x g", g=GRID), AX.X, OP.add))
    P.op("vector", lambda q: q.tensor_tensor(DEL[...], S256[...], XL256[...],
                                             OP.mult))
    P.op("vector", lambda q: q.tensor_tensor(DEL[...], DEL[...], B256[...], OP.add))
    P.op("vector", lambda q: q.tensor_tensor(
        DEL[...].rearrange("p (t j) -> p t j", t=8),
        DEL[...].rearrange("p (t j) -> p t j", t=8),
        M8["SC8"][...].unsqueeze(2).broadcast_to([128, 8, DS]), OP.mult))
    P.op("vector", lambda q: q.tensor_tensor(M8["VY8"][...], r2, r2, OP.mult))
    P.op("vector", lambda q: q.tensor_tensor(M8["VY8"][...], M8["VY8"][...], v2,
                                             OP.mult), inc="dve")
    a1 = P.op("scalar", lambda q: q.activation(
        out=M8["R2P8"][...], in_=M8["VY8"][...], func=AF.Sqrt,
        bias=epsc[:, :]), waits=[("dve", P.cnt["dve"])], inc="act")
    P.op("vector", lambda q: q.reciprocal(M8["R2P8"][...], M8["R2P8"][...]),
         waits=[("act", a1)])
    P.op("vector", lambda q: q.reciprocal(GSEL[...], GSEL[...]))
    P.op("vector", lambda q: q.tensor_tensor(Y2S[...], Y2S[...], GSEL[...],
                                             OP.mult))
    P.op("vector", lambda q: q.tensor_tensor(
        Y2S[...].rearrange("p (t j) -> p t j", t=8),
        Y2S[...].rearrange("p (t j) -> p t j", t=8),
        M8["R2P8"][...].unsqueeze(2).broadcast_to([128, 8, DS]), OP.mult))
    P.op("vector", lambda q: q.tensor_reduce(
        M8["MUF8"][...], DEL[...].rearrange("p (t j) -> p t j", t=8), AX.X, OP.add))
    P.op("vector", lambda q: q.tensor_single_scalar(
        M8["MUF8"][...], M8["MUF8"][...], 1.0 / 2048.0, OP.mult))
    P.op("vector", lambda q: q.tensor_tensor(GS256[...], Y2S[...], DEL[...],
                                             OP.mult))
    P.op("vector", lambda q: q.tensor_reduce(
        M8["CR8"][...], GS256[...].rearrange("p (t j) -> p t j", t=8), AX.X, OP.add))
    P.op("vector", lambda q: q.tensor_tensor(GS256[...], DEL[...], DEL[...],
                                             OP.mult))
    P.op("vector", lambda q: q.tensor_reduce(
        M8["DD8"][...], GS256[...].rearrange("p (t j) -> p t j", t=8), AX.X, OP.add))
    P.op("vector", lambda q: q.scalar_tensor_tensor(
        M8["DD8"][...], M8["CR8"][...], 2.0, M8["DD8"][...], OP.mult, OP.add))
    P.op("vector", lambda q: q.tensor_single_scalar(
        M8["DD8"][...], M8["DD8"][...], 1.0 / 2048.0, OP.mult))
    P.op("vector", lambda q: q.tensor_tensor(M8["DD8"][...], M8["DD8"][...],
                                             M8["VY8"][...], OP.add))
    P.op("vector", lambda q: q.tensor_tensor(M8["T8"][...], M8["MUF8"][...],
                                             M8["MUF8"][...], OP.mult))
    P.op("vector", lambda q: q.tensor_tensor(M8["DD8"][...], M8["DD8"][...],
                                             M8["T8"][...], OP.subtract), inc="dve")
    a2 = P.op("scalar", lambda q: q.activation(
        out=M8["RF8"][...], in_=M8["DD8"][...], func=AF.Sqrt, bias=epsc[:, :]),
        waits=[("dve", P.cnt["dve"])], inc="act")
    P.op("vector", lambda q: q.reciprocal(M8["RF8"][...], M8["RF8"][...]),
         waits=[("act", a2)])
    P.op("vector", lambda q: q.tensor_tensor(M8["SCL8"][...], M8["RF8"][...], r2,
                                             OP.mult))
    P.op("vector", lambda q: q.tensor_single_scalar(
        M8["SCL8"][...], M8["SCL8"][...], 1.0, OP.add))
    P.op("vector", lambda q: q.tensor_tensor(M8["BO8"][...], mu2, r2, OP.mult))
    P.op("vector", lambda q: q.tensor_tensor(M8["BO8"][...], M8["BO8"][...],
                                             M8["MUF8"][...], OP.add))
    P.op("vector", lambda q: q.tensor_tensor(M8["BO8"][...], M8["BO8"][...],
                                             M8["RF8"][...], OP.mult))
    P.op("vector", lambda q: q.tensor_single_scalar(
        M8["BO8"][...], M8["BO8"][...], -1.0, OP.mult))
    delrf = P.op("vector", lambda q: q.tensor_tensor(
        DEL[...].rearrange("p (t j) -> p t j", t=8),
        DEL[...].rearrange("p (t j) -> p t j", t=8),
        M8["RF8"][...].unsqueeze(2).broadcast_to([128, 8, DS]), OP.mult),
        inc="dve")

    out_dma = []
    ds_done = {}
    for tb in range(8):
        wa = [("dve", delrf), ("ld", x2_ld[tb])]
        if tb >= 2:
            wa.append(("o%d" % (tb % 2), out_dma[tb - 2]))
        av = P.op("scalar", lambda q, tb=tb: q.activation(
            out=OUTT[tb % 2][...], in_=x2t(tb), func=AF.Identity,
            scale=M8["SCL8"][:, tb:tb + 1], bias=M8["BO8"][:, tb:tb + 1]),
            waits=wa, inc="act", drain=False)
        wd = [("dve", delrf)]
        if tb >= 2:
            wd.append(("gp", ds_done[tb - 2]))
        dsv = P.op("vector", lambda q, tb=tb: q.tensor_tensor(
            SC2x[tb % 2][...].rearrange("p (i j) -> p i j", i=64),
            TOA[:, tb * 64:(tb + 1) * 64].unsqueeze(2).broadcast_to([128, 64, DS]),
            DEL[:, tb * DS:(tb + 1) * DS].unsqueeze(1).broadcast_to([128, 64, DS]),
            OP.mult), waits=wd, inc="dve", drain=False)
        ds_done[tb] = P.op("gpsimd", lambda q, tb=tb: q.tensor_add(
            OUTT[tb % 2][...], OUTT[tb % 2][...], SC2x[tb % 2][...]),
            waits=[("act", av), ("dve", dsv)], inc="gp")
        out_dma.append(P.dma("sync", dout[tb * 128:(tb + 1) * 128, :],
                             OUTT[tb % 2][...], "o%d" % (tb % 2),
                             waits=[("gp", ds_done[tb])]))
    P.op("sync", lambda q: q.nop(),
         waits=[("o0", P.cnt["o0"]), ("o1", P.cnt["o1"])])
    P.emit()
    return nc, din, dout, dbg


def _host_prep(inputs):
    x = np.asarray(inputs["L0_attn_x"], np.float32)
    Wq = np.asarray(inputs["L0_attn_Wq"], np.float32)
    Wk = np.asarray(inputs["L0_attn_Wk"], np.float32)
    Wv = np.asarray(inputs["L0_attn_Wv"], np.float32)
    Wo = np.asarray(inputs["L0_attn_Wo"], np.float32)
    bo = np.asarray(inputs["L0_attn_bo"], np.float32)
    gam = np.asarray(inputs["L0_ffn_ln_gamma"], np.float32)
    bet = np.asarray(inputs["L0_ffn_ln_beta"], np.float32)
    bas = np.asarray(inputs["L0_ffn_spline_bases"], np.float32)
    slo = np.asarray(inputs["L0_ffn_spline_slopes"], np.float32)
    osc = np.asarray(inputs["L0_ffn_output_scale"], np.float32)

    Wq_t = np.ascontiguousarray(
        Wq.reshape(16, 128, 16, 128).transpose(2, 1, 0, 3).reshape(16, 128, 2048))
    Wk_t = np.ascontiguousarray(
        Wk.reshape(16, 128, 4, 128).transpose(2, 1, 0, 3).reshape(4, 128, 2048))

    qoff = np.arange(128)[None, :]
    koff = np.arange(128)[:, None]
    tri = np.where(qoff >= koff, 0.0, NEG).astype(bfdt)
    common1 = {
        "Wq_t": Wq_t, "Wk_t": Wk_t, "Wv": Wv, "Wo": Wo,
        "ident_bf": np.eye(128, dtype=bfdt).view(np.uint16),
        "ident_f32": np.eye(128, dtype=np.float32),
        "one11": np.ones((1, 1), np.float32),
        "tri": tri.view(np.uint16),
        "ones_col_bf": np.ones((128, 1), bfdt).view(np.uint16),
        "ones_row_f32": np.ones((1, 128), np.float32),
        "iota64": np.tile(np.arange(64, dtype=np.float32)[None, :], (128, 1)),
        "iota8": np.tile(np.arange(8, dtype=np.float32)[None, :], (128, 1)),
        "bo": np.ascontiguousarray(bo.reshape(1, D)),
        "gam": np.ascontiguousarray(gam.reshape(1, D)),
        "bet": np.ascontiguousarray(bet.reshape(1, D)),
        "slopes": np.ascontiguousarray(slo.reshape(NT, DS * GRID)),
    }
    common2 = {
        "ident_f32": common1["ident_f32"],
        "grid_cmp": np.tile((np.arange(GRID, dtype=np.float32) / GRID)[None, :],
                            (128, DS)).reshape(128, DS * GRID),
        "bases": np.ascontiguousarray(bas.reshape(NT, DS * GRID)),
        "slopes": common1["slopes"],
        "oscale": np.ascontiguousarray(osc.reshape(NT, 1)),
        "gam_rs": np.ascontiguousarray(gam.reshape(NT, DS)),
        "bet_rs": np.ascontiguousarray(bet.reshape(NT, DS)),
    }

    in1, in2base = [], []
    for c in range(NC):
        b, par = c // 2, c % 2
        xl = x[b].reshape(16, 128, D)
        if par:
            xl = xl[[i ^ 1 for i in range(16)]]
        xl = np.ascontiguousarray(xl.reshape(T, D))
        m = dict(common1)
        m["xT"] = np.ascontiguousarray(xl.T)
        m["xq"] = np.ascontiguousarray(xl.reshape(16, 128, D)[0::2].reshape(TQ, D))
        m["parity_mask"] = np.full((128, 128), 0.0 if par else NEG,
                                   dtype=bfdt).view(np.uint16)
        bm = np.zeros((8, 4, 128, 128), np.float32)
        for kbp in range(8):
            for j in range(4):
                if kbp < 2 * j:
                    bm[kbp, j] = 1.0
                elif kbp == 2 * j:
                    bm[kbp, j] = (qoff >= koff).astype(np.float32)
                elif kbp == 2 * j + 1:
                    bm[kbp, j] = 1.0 if par else 0.0
        m["bmask"] = np.ascontiguousarray(
            bm.transpose(2, 0, 1, 3).reshape(128, 8 * 512).astype(bfdt)
        ).view(np.uint16)
        in1.append(m)
        in2base.append(dict(common2))
    return in1, in2base


def kernel(**inputs):
    in1, in2base = _host_prep(inputs)
    nc1, _, _, _ = build_launch1()
    r1 = run_bass_kernel_spmd(nc1, in1, core_ids=list(range(NC))).results

    partials = np.stack([np.asarray(r1[c]["o_partials"])[:, 0] for c in range(NC)],
                        axis=1).astype(np.float32)
    in2 = []
    for c in range(NC):
        m = dict(in2base[c])
        m["x2"] = np.ascontiguousarray(np.asarray(r1[c]["o_x2"], np.float32))
        m["partials"] = np.ascontiguousarray(partials)
        m["xsl"] = np.ascontiguousarray(np.asarray(r1[c]["o_xsl"], np.float32))
        m["to_all"] = np.ascontiguousarray(np.asarray(r1[c]["o_to"], np.float32))
        m["onehotT"] = np.ascontiguousarray(np.asarray(r1[c]["o_onehotT"], np.float32))
        m["stats"] = np.ascontiguousarray(np.asarray(r1[c]["o_stats"], np.float32))
        in2.append(m)
    nc2, _, _, _ = build_launch2()
    r2 = run_bass_kernel_spmd(nc2, in2, core_ids=list(range(NC))).results

    out = np.empty((B, T, D), np.float32)
    for c in range(NC):
        b, par = c // 2, c % 2
        o = np.asarray(r2[c]["o_out"], np.float32).reshape(8, 128, D)
        for s in range(8):
            out[b, (2 * s + par) * 128:(2 * s + par + 1) * 128, :] = o[s]
    return out
